# version 02
# baseline (speedup 1.0000x reference)
import numpy as np
from contextlib import ExitStack
import concourse.bass as bass
import concourse.mybir as mybir

F32 = mybir.dt.float32
BF16 = mybir.dt.bfloat16
AF = mybir.ActivationFunctionType
ALU = mybir.AluOpType
AX = mybir.AxisListType

L, D, DIN = 2048, 2048, 4096
H, PDIM, NS, CHK = 64, 64, 128, 128
DCC, NCC, DPROJ = 4352, 34, 8512
TILE, NT, TPC = 512, 4, 4
NE, EPS = 8, 1e-5
TG, NTG = 1024, 2

ENGS = ("tensor", "vector", "scalar", "gpsimd", "sync")


class Sched:
    def __init__(self, nc, ctx):
        names = ["PE", "DV", "AC", "GP", "WG", "SC"]
        self.sem = {n: ctx.enter_context(nc.semaphore(n)) for n in names}
        self.cnt = {n: 0 for n in names}
        self.streams = {e: [] for e in ENGS}
        self.engsem = {"tensor": "PE", "vector": "DV", "scalar": "AC",
                       "gpsimd": "GP", "sync": None}

    def _flat(self, deps):
        dl = []
        for d in deps:
            if d is None:
                continue
            if isinstance(d, list):
                dl += [x for x in d if x is not None]
            else:
                dl.append(d)
        return tuple(dl)

    def op(self, eng, fn, deps=(), inc=False):
        dl = self._flat(deps)
        if inc:
            semn = self.engsem[eng]
            self.cnt[semn] += 1
            self.streams[eng].append((dl, fn, semn, 1))
            return (semn, self.cnt[semn])
        self.streams[eng].append((dl, fn, None, 1))
        return None

    def dma(self, eng, out, in_, deps=(), slow=False):
        s = "WG" if eng == "gpsimd" else "SC"
        def fn(e, out=out, in_=in_, slow=slow):
            return e.dma_start(out=out, in_=in_, allow_slow_non_contiguous=slow)
        self.cnt[s] += 16
        self.streams[eng].append((self._flat(deps), fn, s, 16))
        return (s, self.cnt[s])

    def wait(self, eng, deps):
        self.op(eng, lambda e: None, deps=deps)

    def replay(self, block):
        def make(engname):
            maxw = {}
            own = self.engsem[engname]
            stream = self.streams[engname]
            sems = self.sem

            def body(eobj):
                for dl, fn, isem, incn in stream:
                    for (sn, v) in dl:
                        if sn == own:
                            continue
                        if maxw.get(sn, 0) >= v:
                            continue
                        eobj.wait_ge(sems[sn], v)
                        maxw[sn] = v
                    instr = fn(eobj)
                    if isem is not None:
                        assert instr is not None, f"inc on None instr ({engname})"
                        instr.then_inc(sems[isem], incn)
            return body
        block.tensor(make("tensor"))
        block.vector(make("vector"))
        block.scalar(make("scalar"))
        block.gpsimd(make("gpsimd"))
        block.sync(make("sync"))


class Banks:
    def __init__(self):
        self.w = {}
        self.r = {}

    def wdeps(self, *banks):
        out = []
        for b in banks:
            out += list(self.r.get(b, {}).values())
        return out

    def wrote(self, ev, *banks):
        for b in banks:
            self.w[b] = ev
            self.r[b] = {}

    def rdeps(self, *banks):
        return [self.w.get(b) for b in banks]

    def read(self, ev, *banks):
        for b in banks:
            self.r.setdefault(b, {})[ev[0]] = ev


class K:
    pass


BK = lambda i, h: (i, h)


def SP(k, v):
    v.tensor_copy(k.spc[:, 128:256], k.spc[:, 0:128])


def build_nc(debug=0, stage=9, dbg_ci=(0, 5), races=True):
    nc = bass.Bass("TRN2", target_bir_lowering=False, debug=False,
                   detect_race_conditions=races)
    k = K()
    k.debug, k.stage, k.dbg_ci = debug, stage, tuple(dbg_ci)
    k.u = nc.dram_tensor("u", [L, D], F32, kind="ExternalInput").ap()
    k.w_in = nc.dram_tensor("w_in", [D, DPROJ], F32, kind="ExternalInput").ap()
    k.conv_w = nc.dram_tensor("conv_w", [DCC, 4], F32, kind="ExternalInput").ap()
    k.conv_b = nc.dram_tensor("conv_b", [DCC], F32, kind="ExternalInput").ap()
    k.dt_bias = nc.dram_tensor("dt_bias", [H], F32, kind="ExternalInput").ap()
    k.a_log = nc.dram_tensor("a_log", [H], F32, kind="ExternalInput").ap()
    k.dvec = nc.dram_tensor("dvec", [H], F32, kind="ExternalInput").ap()
    k.norm_w = nc.dram_tensor("norm_w", [DIN], F32, kind="ExternalInput").ap()
    k.w_out = nc.dram_tensor("w_out", [DIN, D], F32, kind="ExternalInput").ap()
    k.gate_w = nc.dram_tensor("gate_w", [NE, D], F32, kind="ExternalInput").ap()
    k.exp_w = nc.dram_tensor("exp_w", [NE, D, D], F32, kind="ExternalInput").ap()
    k.exp_b = nc.dram_tensor("exp_b", [NE, D], F32, kind="ExternalInput").ap()
    k.out = nc.dram_tensor("out", [L, D], F32, kind="ExternalOutput").ap()
    scrkind = "ExternalOutput" if debug else "Internal"
    k.xn_scr = nc.dram_tensor("xn_scr", [D, L // 2], F32, kind=scrkind).ap().bitcast(BF16)
    k.z_scr = nc.dram_tensor("z_scr", [L, DIN // 2], F32, kind=scrkind).ap().bitcast(BF16)
    k.att_scr = nc.dram_tensor("att_scr", [D, L], F32, kind=scrkind).ap()
    if debug:
        k.dbg_g = nc.dram_tensor("dbg_g", [NE, L], F32, kind="ExternalOutput").ap()
        k.dbg_xtok = nc.dram_tensor("dbg_xtok", [128, 8192], F32, kind="ExternalOutput").ap().bitcast(BF16)
        k.dbg_y = nc.dram_tensor("dbg_y", [2 * 128, DIN], F32, kind="ExternalOutput").ap()
        k.dbg_p = nc.dram_tensor("dbg_p", [128, 4096], F32, kind="ExternalOutput").ap()
        k.dbg_st = nc.dram_tensor("dbg_st", [2 * 128, DIN], F32, kind="ExternalOutput").ap()

    ctx = ExitStack()
    with ctx:
        sb = lambda name, shp, dt: ctx.enter_context(nc.sbuf_tensor(name, shp, dt))
        k.P = [ctx.enter_context(nc.psum_tensor(f"P{i}", [128, 1024], F32)) for i in range(4)]
        k.pb = Banks()

        arz = sb("arz", [128, 16384], BF16)
        ar1 = sb("ar1", [128, 16384], BF16)
        ar2b = sb("ar2b", [128, 8192], BF16)
        ar4 = sb("ar4", [128, 8192], BF16)
        ar5 = sb("ar5", [128, 4096], F32)
        ar6 = sb("ar6", [128, 8192], BF16)
        ar7 = sb("ar7", [128, 1024], F32)
        ar8 = sb("ar8", [128, 2048], BF16)
        k.buf2 = sb("buf2", [128, 16384], BF16)
        k.state = sb("state", [128, DIN], F32)

        k.z_stage = [arz[:, i * 4096:(i + 1) * 4096] for i in range(4)]
        k.y_bf = arz[:, 0:4096]
        k.th_bf = arz[:, 4096:8192]
        k.d8 = [arz[:, 8192 + i * 2048: 8192 + (i + 1) * 2048].bitcast(F32) for i in range(2)]
        k.g8_bf = arz[:, 0:8192]
        k.att_rd = [arz[:, 8192 + i * 2048: 8192 + (i + 1) * 2048].bitcast(F32) for i in range(2)]
        k.xn_tile = ar1[:, 0:8192]
        k.wib = [ar1[:, 8192 + i * 4096: 8192 + (i + 1) * 4096] for i in range(2)]
        k.xn_tg = ar1[:, :]
        k.sel64 = ar2b[0:64, :]
        k.rhs_ek = [ar2b[:, i * 1024:(i + 1) * 1024] for i in range(8)]
        k.z_rd = [ar4[:, i * 4096:(i + 1) * 4096] for i in range(2)]
        k.web = [ar4[:, i * 4096:(i + 1) * 4096] for i in range(2)]
        k.u_sb = [ar5[:, i * 2048:(i + 1) * 2048] for i in range(2)]
        k.st_bf = [ar5[:, i * 256:(i + 1) * 256].bitcast(BF16) for i in range(2)]
        k.m8 = [ar5[:, 512 + i * 512: 512 + (i + 1) * 512].bitcast(BF16) for i in range(2)]
        k.wb8 = [ar5[:, 1536 + i * 512: 1536 + (i + 1) * 512].bitcast(BF16) for i in range(2)]
        k.e8 = ar5[:, 2560:3072].bitcast(BF16)
        k.id8 = ar5[:, 3072:3584].bitcast(BF16)
        k.cbm_bf = ar5[:, 3584:3648].bitcast(BF16)
        k.usq = ar6[:, 0:4096].bitcast(F32)
        k.xn_tok = ar6[:, 4096:6144]
        k.xn_stage = ar6[:, 6144:8192]
        k.wob = [ar6[:, i * 4096:(i + 1) * 4096] for i in range(2)]
        k.att_st = [ar7[:, i * 512:(i + 1) * 512] for i in range(2)]
        k.eb_bf = ar7[0:8, :].bitcast(BF16)
        k.gw_bf = ar8[0:8, :]
        k.zf_bf = [ar8[:, i * 512:(i + 1) * 512] for i in range(2)]
        k.xf_bf = [ar8[:, 1024 + i * 512: 1024 + (i + 1) * 512] for i in range(2)]
        k.out_st = [k.state[:, i * 2048:(i + 1) * 2048] for i in range(2)]
        k.outf = k.buf2[:, :]
        k.zst = [k.wob[i // 2][:, (i % 2) * 2048:(i % 2 + 1) * 2048] for i in range(4)]
        k.pzT = None
        k.t_bf = k.y_bf

        k.ones1 = sb("ones1", [1, 128], F32)
        k.one64 = sb("one64", [64, 1], F32)
        k.ident_bf = sb("ident_bf", [128, 128], BF16)
        k.ident_f = sb("ident_f", [128, 128], F32)
        k.tri01 = sb("tri01", [128, 128], BF16)
        k.sel8 = sb("sel8", [8, 1024], BF16)
        k.eps_col = sb("eps_col", [128, 1], F32)
        k.cwcol = sb("cwcol", [128, 136], F32)
        k.cb_col = sb("cb_col", [128, 34], F32)
        k.dtb_col = sb("dtb_col", [64, 1], F32)
        k.a_col = sb("a_col", [64, 1], F32)
        k.d_row = sb("d_row", [1, 64], F32)
        k.dbc = sb("dbc", [128, 64], F32)
        k.nw_col = sb("nw_col", [128, 32], F32)
        k.gwt_bf = sb("gwt_bf", [128, 128], BF16)
        k.gates_bf = sb("gates_bf", [8, 2048], BF16)
        k.lg_all = sb("lg_all", [128, 128], F32)
        k.gt_small = sb("gt_small", [128, 16], F32)
        k.gates_tok = sb("gates_tok", [128, 8], F32)
        k.st1 = sb("st1", [128, 8], F32)
        k.spc = sb("spc", [128, 256], F32)
        k.pzT = k.P[0] if False else None
        k.wibdt = k.wib[1][:, 0:1024]
        k.rstd_bc = k.wib[0][:, 0:1024].bitcast(F32)
        k.xraw = [sb(f"xraw{i}", [128, 520], BF16) for i in range(2)]
        k.halo = sb("halo", [128, 102], BF16)
        k.wdg = [sb(f"wdg{i}", [128, 128], BF16) for i in range(4)]
        k.bf_bf = sb("bf_bf", [128, 512], BF16)
        k.cf_bf = sb("cf_bf", [128, 512], BF16)
        k.btok_bf = sb("btok_bf", [128, 512], BF16)
        k.dta0 = k.att_st[1][0:64, 0:512]
        k.dta1 = k.att_st[0][0:64, 0:512]
        k.dtf = k.wob[1][0:64, 0:1024].bitcast(F32)
        k.wtmp = k.wob[0][0:64, 0:1024].bitcast(F32)
        k.cum_hi = sb("cum_hi", [64, 512], BF16)
        k.cum_lo = sb("cum_lo", [64, 512], BF16)
        k.ncum_hi = sb("ncum_hi", [64, 512], BF16)
        k.ncum_lo = sb("ncum_lo", [64, 512], BF16)
        k.expcum_t = sb("expcum_t", [128, 256], F32)
        k.cum_t = k.expcum_t
        k.dtft_bf = None
        k.wt_bf = sb("wt_bf", [128, 256], BF16)
        k.dtft_bf = k.lg_all[:, :].bitcast(BF16)
        k.atot4 = sb("atot4", [64, 4], F32)

        k.atot_bc = sb("atot_bc", [128, 256], F32)
        k.ssq_t = sb("ssq_t", [128, 8], F32)
        k.rstd_t = sb("rstd_t", [128, 4], F32)
        k.rstd_row = None
        k.rstd_row = arz[0:1, 8192:9216].bitcast(F32)
        k.atot_row = k.rstd_row[0:1, 0:256]

        k.p = Sched(nc, ctx)
        block = ctx.enter_context(nc.Block())

        phase0(k)
        phase1(k)
        if stage >= 2:
            phase2(k)
        if stage >= 4:
            phase3(k)

        k.p.wait("sync", [("SC", k.p.cnt["SC"]), ("WG", k.p.cnt["WG"])])
        k.p.replay(block)
    return nc


def phase0(k):
    p, pb, P = k.p, k.pb, k.P
    k.ev_cw = p.dma("sync", k.cwcol[:].rearrange("p (cc kk) -> p cc kk", kk=4),
                    k.conv_w.rearrange("(cc p) kk -> p cc kk", p=128))
    k.ev_cb = p.dma("sync", k.cb_col[:], k.conv_b.rearrange("(cc p) -> p cc", p=128), slow=True)
    k.ev_dtb = p.dma("sync", k.dtb_col[:], k.dt_bias.rearrange("h -> h ()"))
    ev_alog = p.dma("sync", k.a_col[:], k.a_log.rearrange("h -> h ()"))
    ev_dv = p.dma("sync", k.d_row[:], k.dvec.rearrange("h -> () h"))
    k.ev_nw = p.dma("sync", k.nw_col[:], k.norm_w.rearrange("(kk p) -> p kk", p=128), slow=True)
    ev_gw = p.dma("gpsimd", k.gw_bf[:], k.gate_w)

    def g_ops(g):
        g.memset(k.ones1[:], 1.0)
        g.memset(k.one64[:], 1.0)
        g.memset(k.eps_col[:], EPS)
        g.memset(k.ident_bf[:], 0.0)
        g.affine_select(out=k.ident_bf[:], in_=k.ident_bf[:], compare_op=ALU.not_equal,
                        fill=1.0, base=0, pattern=[[-1, 128]], channel_multiplier=1)
        g.memset(k.ident_f[:], 0.0)
        g.affine_select(out=k.ident_f[:], in_=k.ident_f[:], compare_op=ALU.not_equal,
                        fill=1.0, base=0, pattern=[[-1, 128]], channel_multiplier=1)
        g.memset(k.tri01[:], 1.0)
        g.affine_select(out=k.tri01[:], in_=k.tri01[:], compare_op=ALU.is_ge,
                        fill=0.0, base=0, pattern=[[1, 128]], channel_multiplier=-1)
        g.memset(k.sel64[:], 0.0)
        g.affine_select(out=k.sel64[:].rearrange("kq (h m) -> kq h m", m=128),
                        in_=k.sel64[:].rearrange("kq (h m) -> kq h m", m=128),
                        compare_op=ALU.not_equal, fill=1.0, base=0,
                        pattern=[[-1, 64], [0, 128]], channel_multiplier=1)
        g.memset(k.sel8[:], 0.0)
        g.affine_select(out=k.sel8[:].rearrange("kq (e m) -> kq e m", m=128),
                        in_=k.sel8[:].rearrange("kq (e m) -> kq e m", m=128),
                        compare_op=ALU.not_equal, fill=1.0, base=0,
                        pattern=[[-1, 8], [0, 128]], channel_multiplier=1)
        g.memset(k.halo[:], 0.0)
        g.memset(k.spc[:], 0.0)
        return g.memset(k.state[:], 0.0)
    k.ev_gp0 = p.op("gpsimd", g_ops, inc=True)

    ev0 = p.op("scalar", lambda s: s.activation(out=k.a_col[:], in_=k.a_col[:], func=AF.Exp),
               deps=[ev_alog], inc=True)
    k.ev_acol = p.op("vector", lambda v: v.tensor_scalar(out=k.a_col[:], in0=k.a_col[:],
                                                         scalar1=-1.0, scalar2=None, op0=ALU.mult),
                     deps=[ev0], inc=True)
    evm = p.op("tensor", lambda t: t.matmul(P[0][:, 0:64], k.ones1[0:1, :], k.d_row[0:1, :],
                                            start=True, stop=True),
               deps=[ev_dv, k.ev_gp0] + pb.wdeps(BK(0, 0)), inc=True)
    pb.wrote(evm, BK(0, 0))
    k.ev_dbc = p.op("vector", lambda v: v.tensor_copy(k.dbc[:], P[0][:, 0:64]),
                    deps=pb.rdeps(BK(0, 0)) + [evm], inc=True)
    pb.read(k.ev_dbc, BK(0, 0))
    ev_gwt = None
    for kk in range(16):
        pv = P[1][:, 0:4].bitcast(BF16)
        evt = p.op("tensor",
                   lambda t, pv=pv, kk=kk: t.transpose(pv, k.gw_bf[0:8, kk * 128:(kk + 1) * 128],
                                                       k.ident_bf[0:8, 0:8]),
                   deps=([ev_gw, k.ev_gp0] if kk == 0 else []) + pb.wdeps(BK(1, 0)), inc=True)
        pb.wrote(evt, BK(1, 0))
        ev_gwt = p.op("vector",
                      lambda v, pv=pv, kk=kk: v.tensor_copy(k.gwt_bf[:, kk * 8:(kk + 1) * 8], pv),
                      deps=pb.rdeps(BK(1, 0)) + [evt], inc=True)
        pb.read(ev_gwt, BK(1, 0))
    k.ev_gwt = ev_gwt


def phase1(k):
    p, pb, P = k.p, k.pb, k.P
    p.op("scalar", lambda s: s.preload_act_table(AF.Sqrt))
    k.ev_xn_dma = []
    ev_lg = []
    ev_u_free = [None, None]
    ev_xnw_prev = None
    ev_lgmm_prev = None
    ev_xnT_prev = None
    ev_msq_prev = None
    for tc in range(16):
        sl = tc % 2
        ev_u = p.dma("sync", k.u_sb[sl][:], k.u[tc * 128:(tc + 1) * 128, :], deps=[ev_u_free[sl]])
        ev_mean = p.op("vector", lambda v, sl=sl: v.tensor_reduce(
            out=k.st1[:, 0:1], in_=k.u_sb[sl][:], axis=AX.X, op=ALU.add), deps=[ev_u], inc=True)
        ev_sq = p.op("scalar", lambda s, sl=sl: s.activation(out=k.usq[:], in_=k.u_sb[sl][:],
                                                             func=AF.Square),
                     deps=[ev_u, ev_msq_prev], inc=True)
        ev_msq = p.op("vector", lambda v: v.tensor_reduce(out=k.st1[:, 1:2], in_=k.usq[:],
                                                          axis=AX.X, op=ALU.add),
                      deps=[ev_sq], inc=True)
        ev_msq_prev = ev_msq
        def v_stats(v):
            v.tensor_scalar(out=k.st1[:, 0:1], in0=k.st1[:, 0:1], scalar1=1.0 / D, scalar2=None,
                            op0=ALU.mult)
            v.tensor_scalar(out=k.st1[:, 1:2], in0=k.st1[:, 1:2], scalar1=1.0 / D, scalar2=None,
                            op0=ALU.mult)
            v.tensor_tensor(out=k.st1[:, 2:3], in0=k.st1[:, 0:1], in1=k.st1[:, 0:1],
                            op=ALU.mult)
            SP(k, v)
            return v.scalar_tensor_tensor(out=k.st1[:, 2:3], in0=k.st1[:, 2:3], scalar=-1.0,
                                          in1=k.st1[:, 1:2], op0=ALU.mult, op1=ALU.add)
        ev_var = p.op("vector", v_stats, deps=[ev_msq], inc=True)
        ev_sqv = p.op("scalar", lambda s: s.activation(out=k.st1[:, 3:4], in_=k.st1[:, 2:3],
                                                       func=AF.Sqrt, bias=k.eps_col[:]),
                      deps=[ev_var], inc=True)
        ev_rst = p.op("vector", lambda v: v.reciprocal(k.st1[:, 3:4], k.st1[:, 3:4]),
                      deps=[ev_sqv], inc=True)
        def v_xn(v, sl=sl):
            SP(k, v)
            return v.tensor_scalar(
                out=k.xn_tok[:], in0=k.u_sb[sl][:], scalar1=k.st1[:, 0:1], scalar2=k.st1[:, 3:4],
                op0=ALU.subtract, op1=ALU.mult)
        ev_xn = p.op("vector", v_xn, deps=[ev_rst, ev_xnT_prev], inc=True)
        ev_u_free[sl] = ev_xn
        ev_ev4 = []
        for q in range(4):
            bk = BK(2, q % 2)
            pvq = P[2][:, (q % 2) * 512:(q % 2) * 512 + 256].bitcast(BF16)
            evt = None
            for j in range(4):
                kk = q * 4 + j
                evt = p.op("tensor",
                           lambda t, pvq=pvq, j=j, kk=kk: t.transpose(
                               pvq[:, j * 128:(j + 1) * 128],
                               k.xn_tok[:, kk * 128:(kk + 1) * 128], k.ident_bf[:]),
                           deps=([ev_xn] + pb.wdeps(bk)) if j == 0 else [], inc=(j == 3))
            pb.wrote(evt, bk)
            if q == 3:
                ev_xnT_prev = evt
            eve = p.op("vector",
                       lambda v, pvq=pvq, q=q: v.tensor_copy(
                           k.xn_stage[:].rearrange("p (kk t) -> p kk t", t=128)[:, q * 4:(q + 1) * 4, :],
                           pvq.rearrange("p (kk t) -> p kk t", t=128)),
                       deps=pb.rdeps(bk) + [evt, ev_xnw_prev if q == 0 else None,
                                            ev_lgmm_prev if q == 0 else None], inc=True)
            pb.read(eve, bk)
            ev_ev4.append(eve)
        ev_w = p.dma("sync", k.xn_scr.rearrange("(kk p) t -> p kk t", p=128)[:, :, tc * 128:(tc + 1) * 128],
                     k.xn_stage[:].rearrange("p (kk t) -> p kk t", t=128), deps=ev_ev4)
        k.ev_xn_dma.append(ev_w)
        ev_xnw_prev = ev_w
        evl = None
        for kk in range(16):
            evl = p.op("tensor",
                       lambda t, kk=kk: t.matmul(P[3][:, 0:8], k.xn_stage[:, kk * 128:(kk + 1) * 128],
                                                 k.gwt_bf[:, kk * 8:(kk + 1) * 8],
                                                 start=(kk == 0), stop=(kk == 15)),
                       deps=([ev_ev4[-1], k.ev_gwt] + pb.wdeps(BK(3, 0))) if kk == 0 else [],
                       inc=(kk == 15))
        pb.wrote(evl, BK(3, 0))
        ev_lgmm_prev = evl
        evle = p.op("vector", lambda v, tc=tc: v.tensor_copy(k.lg_all[:, tc * 8:(tc + 1) * 8],
                                                             P[3][:, 0:8]),
                    deps=pb.rdeps(BK(3, 0)) + [evl], inc=True)
        pb.read(evle, BK(3, 0))
        ev_lg.append(evle)

    p.op("scalar", lambda s: s.preload_act_table(AF.Exp))
    ev_gts = None
    ev_gT_prev = None
    for tc in range(16):
        lg = k.lg_all[:, tc * 8:(tc + 1) * 8]
        def v_g1(v, lg=lg):
            v.tensor_reduce(out=k.gt_small[:, 0:1], in_=lg, axis=AX.X, op=ALU.max)
            SP(k, v)
            v.tensor_scalar(out=k.gates_tok[:], in0=lg, scalar1=k.gt_small[:, 0:1],
                            scalar2=None, op0=ALU.is_equal)
            SP(k, v)
            v.scalar_tensor_tensor(out=k.gt_small[:, 8:16], in0=k.gates_tok[:], scalar=-1e9,
                                   in1=lg, op0=ALU.mult, op1=ALU.add)
            SP(k, v)
            v.tensor_reduce(out=k.gt_small[:, 1:2], in_=k.gt_small[:, 8:16], axis=AX.X, op=ALU.max)
            SP(k, v)
            return v.tensor_scalar(out=k.gt_small[:, 2:3], in0=k.gt_small[:, 1:2],
                                   scalar1=k.gt_small[:, 0:1], scalar2=None, op0=ALU.subtract)
        ev1 = p.op("vector", v_g1, deps=[ev_lg[tc], ev_gT_prev], inc=True)
        ev3 = p.op("scalar", lambda s: s.activation(out=k.gt_small[:, 3:4], in_=k.gt_small[:, 2:3],
                                                    func=AF.Exp), deps=[ev1], inc=True)
        def v_g2(v):
            v.tensor_scalar(out=k.gt_small[:, 3:4], in0=k.gt_small[:, 3:4], scalar1=1.0,
                            scalar2=None, op0=ALU.add)
            SP(k, v)
            v.reciprocal(k.gt_small[:, 4:5], k.gt_small[:, 3:4])
            v.tensor_scalar(out=k.gt_small[:, 8:16], in0=k.gt_small[:, 8:16],
                            scalar1=k.gt_small[:, 1:2], scalar2=None, op0=ALU.is_equal)
            v.tensor_scalar(out=k.gt_small[:, 5:6], in0=k.gt_small[:, 4:5], scalar1=-1.0,
                            scalar2=1.0, op0=ALU.mult, op1=ALU.add)
            v.tensor_scalar(out=k.gates_tok[:], in0=k.gates_tok[:], scalar1=k.gt_small[:, 4:5],
                            scalar2=None, op0=ALU.mult)
            SP(k, v)
            return v.scalar_tensor_tensor(out=k.gates_tok[:], in0=k.gt_small[:, 8:16],
                                          scalar=k.gt_small[:, 5:6], in1=k.gates_tok[:],
                                          op0=ALU.mult, op1=ALU.add)
        ev4 = p.op("vector", v_g2, deps=[ev3], inc=True)
        evt = p.op("tensor", lambda t: t.transpose(P[3][0:8, 512:640], k.gates_tok[:], k.ident_f[:]),
                   deps=[ev4] + pb.wdeps(BK(3, 1)), inc=True)
        pb.wrote(evt, BK(3, 1))
        ev_gT_prev = evt
        ev_gts = p.op("vector", lambda v, tc=tc: v.tensor_copy(
            k.gates_bf[:, tc * 128:(tc + 1) * 128], P[3][0:8, 512:640]),
            deps=pb.rdeps(BK(3, 1)) + [evt], inc=True)
        pb.read(ev_gts, BK(3, 1))
    k.ev_gts = ev_gts
    if k.debug:
        evc = p.op("vector", lambda v: v.tensor_copy(k.usq[0:8, 0:2048], k.gates_bf[:]),
                   deps=[ev_gts], inc=True)
        p.dma("sync", k.dbg_g, k.usq[0:8, 0:2048], deps=[evc])


def phase2(k):
    p, pb, P = k.p, k.pb, k.P
    S = k
    ev_wib_free = [None, None]
    ev_wibdt_free = None
    ev_xnt_free = None
    ev_zf_free = [k.ev_gwt, k.ev_gwt]
    ev_zst_free = [None] * 8
    ev_zrd_free = [None, None]
    ev_xraw_free = [None, None]
    ev_wdg_free = [None] * 4
    ev_xfbf_free = [k.ev_gwt, k.ev_gwt]
    ev_d8_free = [None, None]
    ev_e8_free = None
    ev_m8_free = [None, None]
    ev_stbf_free = [None, None]
    ev_wb8_free = [None, None]
    ev_id8_free = None
    ev_wob_free = [None, None]
    ev_attst_free = [None, None]
    ev_wout_done = None
    ev_tT = [None] * TPC
    ev_halo = [k.ev_gp0] * NCC
    ev_bf = ev_cf = ev_btok = None
    ev_state_upd = [k.ev_gp0] * 8
    ev_y_evac = None
    ev_ect_prev = None
    ev_ab_prev = None
    ev_Dmm_prev = None
    ev_sq2_prev = None
    ev_norm_prev = None
    ev_sqact_prev = None
    ev_dtraw = None
    ev_cbm_prev = None
    ev_rh_free = [k.ev_gts]
    evt_ = None
    eve = None
    eve8 = None
    ev_ssq = None
    evat = None
    ev_prep = None
    ev_dtft = None
    ev_wt = None
    ev_spl = None
    ev_w2 = None
    ev_ect = None
    cum = k.dta1
    k.ev_att_w = []
    zdma = [None] * 8
    zrd_ready = {}

    st = {}
    k.pzT = P[3][:, 640:896].bitcast(BF16)

    def zstage(m, tcl):
        r = m // 16
        return k.zst[tcl][:, (m - r * 16) * 128:(m - r * 16 + 1) * 128]

    def emit_win_m(ti, mg, chunkloop):
        nonlocal ev_wib_free, ev_wibdt_free, ev_dtraw, ev_ab_prev, ev_zf_free
        nonlocal ev_xraw_free, ev_wdg_free, ev_xfbf_free, ev_halo, ev_bf, ev_cf
        nonlocal ev_wout_done, ev_tT, ev_cbm_prev, ev_y_evac, eve, ev_zst_free
        nonlocal ev_xnt_free, ev_wob_free
        slw = mg % 2
        if mg < 33:
            ev_w = p.dma("gpsimd", k.wib[slw].rearrange("p (kk m) -> p kk m", m=256),
                         k.w_in.rearrange("(kk p) m -> p kk m", p=128)[:, :, mg * 256:(mg + 1) * 256],
                         deps=[ev_wib_free[slw], k.ev_gts if (ti == 0 and mg < 2) else None,
                               ev_wibdt_free if mg in (0, 1, 16, 17) else None])
            mos = (0, 1)
        else:
            ev_w = p.dma("gpsimd", k.wibdt[:].rearrange("p (kk m) -> p kk m", m=64),
                         k.w_in.rearrange("(kk p) m -> p kk m", p=128)[:, :, 8448:8512],
                         deps=[ev_wibdt_free, ev_wib_free[1]])
            mos = (0,)
        ev_last_mm = None
        for mo in mos:
            m = mg * 2 + mo
            isdt = mg == 33
            if chunkloop:
                slp = m % 2
                bk = BK(0, slp)
                pw = P[0][:, slp * 512:(slp + 1) * 512]
            else:
                slp = m % 4
                bk = BK(slp // 2, slp % 2)
                pw = P[slp // 2][:, (slp % 2) * 512:(slp % 2 + 1) * 512]
            pwv = pw[0:64, :] if isdt else pw
            evm = None
            for kk in range(16):
                lhsT = (k.wibdt[:, kk * 64:(kk + 1) * 64] if isdt
                        else k.wib[slw][:, kk * 256 + mo * 128: kk * 256 + mo * 128 + 128])
                evm = p.op("tensor",
                           lambda t, pwv=pwv, lhsT=lhsT, kk=kk:
                           t.matmul(pwv, lhsT, k.xn_tile[:, kk * 512:(kk + 1) * 512],
                                    start=(kk == 0), stop=(kk == 15)),
                           deps=([ev_w, st.get("ev_xnt")] + pb.wdeps(bk)) if kk == 0 else [],
                           inc=(kk == 15))
            pb.wrote(evm, bk)
            ev_last_mm = evm
            if isdt:
                ev_dtraw = p.op("vector", lambda v, pwv=pwv: v.tensor_copy(k.dta0[:], pwv),
                                deps=pb.rdeps(bk) + [evm, ev_ab_prev,
                                                     ev_attst_free[0], ev_attst_free[1]],
                                inc=True)
                pb.read(ev_dtraw, bk)
                st["ev_dtraw"] = ev_dtraw
            elif m < 32:
                slz = m % 2
                r = m // 16
                evc = p.op("vector", lambda v, pw=pw, slz=slz: v.tensor_copy(k.zf_bf[slz][:], pw),
                           deps=pb.rdeps(bk) + [evm, ev_zf_free[slz]], inc=True)
                pb.read(evc, bk)
                bt = BK(3, 1)
                pzT = k.pzT
                evt = None
                for tcl in range(4):
                    evt = p.op("tensor",
                               lambda t, pzT=pzT, tcl=tcl, slz=slz:
                               t.transpose(pzT[:, tcl * 128:(tcl + 1) * 128],
                                           k.zf_bf[slz][:, tcl * 128:(tcl + 1) * 128],
                                           k.ident_bf[:]),
                               deps=([evc] + pb.wdeps(bt)) if tcl == 0 else [], inc=(tcl == 3))
                pb.wrote(evt, bt)
                ev_zf_free[slz] = evt
                eve = None
                for tcl in range(4):
                    eve = p.op("vector",
                               lambda v, pzT=pzT, tcl=tcl, m=m:
                               v.tensor_copy(zstage(m, tcl), pzT[:, tcl * 128:(tcl + 1) * 128]),
                               deps=(pb.rdeps(bt) + [evt]
                                     + ([ev_zst_free[(1 - m // 16) * 4 + q] for q in range(4)]
                                        + ev_rh_free + [ev_wob_free[0], ev_wob_free[1]]
                                        if m % 16 == 0 else []))
                               if tcl == 0 else [],
                               inc=(tcl == 3))
                pb.read(eve, bt)
                if m % 16 == 15:
                    r = m // 16
                    for tcl in range(4):
                        zdma[r * 4 + tcl] = p.dma(
                            "sync",
                            k.z_scr[ti * 512 + tcl * 128: ti * 512 + (tcl + 1) * 128,
                                    r * 2048:(r + 1) * 2048],
                            k.zst[tcl][:], deps=[eve])
                        ev_zst_free[r * 4 + tcl] = zdma[r * 4 + tcl]
                        zrd_ready.setdefault((ti, tcl), []).append(zdma[r * 4 + tcl])
            else:
                cc = m - 32
                slr = cc % 2
                def v_xbc(v, pw=pw, cc=cc, slr=slr):
                    v.tensor_copy(k.xraw[slr][:, 0:3], k.halo[:, cc * 3:cc * 3 + 3])
                    v.tensor_copy(k.halo[:, cc * 3:cc * 3 + 3], pw[:, 509:512])
                    return v.tensor_copy(k.xraw[slr][:, 3:515], pw)
                evc = p.op("vector", v_xbc,
                           deps=pb.rdeps(bk) + [evm, ev_xraw_free[slr], ev_halo[cc]], inc=True)
                pb.read(evc, bk)
                ev_halo[cc] = evc
                evwd = []
                for kc in range(4):
                    e_ = p.op("vector",
                              lambda v, kc=kc, cc=cc: v.tensor_scalar(
                                  out=k.wdg[kc][:], in0=k.ident_bf[:],
                                  scalar1=k.cwcol[:, cc * 4 + kc:cc * 4 + kc + 1],
                                  scalar2=None, op0=ALU.mult),
                              deps=[ev_wdg_free[kc],
                                    k.ev_cw if (ti == 0 and cc == 0 and kc == 0) else None],
                              inc=True)
                    evwd.append(e_)
                slc = cc % 2
                bc_ = BK(3, slc)
                pcv = P[3][:, slc * 512:(slc + 1) * 512]
                evcm = None
                for kc in range(4):
                    evcm = p.op("tensor",
                                lambda t, pcv=pcv, kc=kc, slr=slr:
                                t.matmul(pcv, k.wdg[kc][:], k.xraw[slr][:, kc:kc + 512],
                                         start=(kc == 0), stop=(kc == 3)),
                                deps=([evc, evwd[3]] + pb.wdeps(bc_)) if kc == 0 else [],
                                inc=(kc == 3))
                pb.wrote(evcm, bc_)
                ev_xraw_free[slr] = evcm
                for kc in range(4):
                    ev_wdg_free[kc] = evcm
                if cc < 32:
                    dst, depx = k.xf_bf[cc % 2], ev_xfbf_free[cc % 2]
                elif cc == 32:
                    dst, depx = k.bf_bf, ev_cbm_prev
                else:
                    dst, depx = k.cf_bf, ev_y_evac
                evse = p.op("scalar",
                            lambda s, dst=dst, pcv=pcv, cc=cc:
                            s.activation(out=dst[:], in_=pcv, func=AF.Silu,
                                         bias=k.cb_col[:, cc:cc + 1]),
                            deps=pb.rdeps(bc_) + [evcm, depx,
                                                  k.ev_cb if (ti == 0 and cc == 0) else None],
                            inc=True)
                pb.read(evse, bc_)
                if cc == 32:
                    ev_bf = evse
                    st["ev_bf"] = evse
                elif cc == 33:
                    ev_cf = evse
                    st["ev_cf"] = evse
                else:
                    slt = cc % 2
                    bt = BK(2, slt)
                    pxT = P[2][:, slt * 512: slt * 512 + 256].bitcast(BF16)
                    evt = None
                    for tcl in range(4):
                        evt = p.op("tensor",
                                   lambda t, pxT=pxT, tcl=tcl, cc=cc:
                                   t.transpose(pxT[:, tcl * 128:(tcl + 1) * 128],
                                               k.xf_bf[cc % 2][:, tcl * 128:(tcl + 1) * 128],
                                               k.ident_bf[:]),
                                   deps=([evse] + pb.wdeps(bt)) if tcl == 0 else [],
                                   inc=(tcl == 3))
                    pb.wrote(evt, bt)
                    ev_xfbf_free[cc % 2] = evt
                    eve = None
                    for tcl in range(4):
                        eve = p.op("vector",
                                   lambda v, pxT=pxT, tcl=tcl, cc=cc:
                                   v.tensor_copy(k.buf2[:, tcl * 4096 + cc * 128: tcl * 4096 + (cc + 1) * 128],
                                                 pxT[:, tcl * 128:(tcl + 1) * 128]),
                                   deps=(pb.rdeps(bt) + [evt, ev_wout_done,
                                                         ev_tT[0], ev_tT[1], ev_tT[2], ev_tT[3]])
                                   if tcl == 0 else [],
                                   inc=(tcl == 3))
                    pb.read(eve, bt)
                    if cc == 0:
                        ev_wout_done = None
                        ev_tT = [None] * TPC
        if mg < 33:
            ev_wib_free[slw] = ev_last_mm
        else:
            ev_wibdt_free = ev_last_mm
            ev_xnt_free = ev_last_mm
            st["ev_xnt_free"] = ev_last_mm

    def emit_prep(ti):
        nonlocal ev_ab_prev, ev_Dmm_prev, ev_ect_prev, ev_prep, ev_dtft, ev_wt
        nonlocal ev_spl, ev_w2, ev_ect, ev_y_evac, ev_wob_free
        ev_dtraw_ = st["ev_dtraw"]
        p.op("scalar", lambda s: s.preload_act_table(AF.Exp))
        ev_t1 = p.op("vector", lambda v: v.tensor_scalar(out=k.dta0[:], in0=k.dta0[:],
                                                         scalar1=k.dtb_col[:], scalar2=None,
                                                         op0=ALU.add),
                     deps=[ev_dtraw_, k.ev_dtb if ti == 0 else None], inc=True)
        ev_ab = p.op("scalar", lambda s: s.activation(out=k.dtf[:], in_=k.dta0[:], func=AF.Abs),
                     deps=[ev_t1, ev_wob_free[1]]
                     + [z for z in zdma[2:4] + zdma[6:8] if z is not None], inc=True)
        ev_ab_prev = ev_ab
        ev_ex = p.op("scalar", lambda s: s.activation(out=k.dtf[:], in_=k.dtf[:], func=AF.Exp,
                                                      scale=-1.0), deps=[ev_ab], inc=True)
        p.op("scalar", lambda s: s.preload_act_table(AF.Ln))
        ev_ln = p.op("scalar", lambda s: s.activation(out=k.dtf[:], in_=k.dtf[:], func=AF.Ln,
                                                      bias=k.one64[:]), deps=[ev_ex], inc=True)
        p.op("scalar", lambda s: s.preload_act_table(AF.Exp))
        def v_dtf(v):
            v.tensor_scalar(out=k.dta0[:], in0=k.dta0[:], scalar1=0.0, scalar2=None, op0=ALU.max)
            return v.tensor_tensor(out=k.dtf[:], in0=k.dtf[:], in1=k.dta0[:], op=ALU.add)
        ev_dtfv = p.op("vector", v_dtf, deps=[ev_ln], inc=True)
        ev_da = p.op("vector", lambda v: v.tensor_scalar(out=k.dta0[:], in0=k.dtf[:],
                                                         scalar1=k.a_col[:], scalar2=None,
                                                         op0=ALU.mult),
                     deps=[ev_dtfv, k.ev_acol if ti == 0 else None, ev_Dmm_prev], inc=True)
        bufs = [k.dta0, k.dta1]
        ev_cs = None
        for si, s_ in enumerate([1, 2, 4, 8, 16, 32, 64]):
            def v_cs(v, s_=s_, si=si):
                src, dst = bufs[si % 2], bufs[1 - si % 2]
                v.tensor_copy(dst[:], src[:])
                return v.tensor_tensor(
                    out=dst[:].rearrange("p (c kq) -> p c kq", kq=128)[:, :, s_:],
                    in0=src[:].rearrange("p (c kq) -> p c kq", kq=128)[:, :, s_:],
                    in1=src[:].rearrange("p (c kq) -> p c kq", kq=128)[:, :, :128 - s_],
                    op=ALU.add)
            ev_cs = p.op("vector", v_cs,
                         deps=[ev_attst_free[0], ev_attst_free[1]] if si == 0 else [],
                         inc=(s_ == 64))
        def v_w1(v):
            SP(k, v)
            e_ = None
            for c in range(4):
                e_ = v.tensor_scalar(out=k.wtmp[:, c * 128:(c + 1) * 128],
                                     in0=cum[:, c * 128:(c + 1) * 128],
                                     scalar1=cum[:, c * 128 + 127:c * 128 + 128],
                                     scalar2=-1.0, op0=ALU.subtract, op1=ALU.mult)
            return e_
        ev_w1 = p.op("vector", v_w1, deps=[ev_cs, ev_wob_free[0]]
                     + [z for z in zdma[0:2] + zdma[4:6] if z is not None], inc=True)
        ev_w2 = p.op("scalar", lambda s: s.activation(out=k.wtmp[:], in_=k.wtmp[:], func=AF.Exp),
                     deps=[ev_w1], inc=True)
        ev_w3 = p.op("vector", lambda v: v.tensor_tensor(out=k.dta0[:], in0=k.wtmp[:], in1=k.dtf[:],
                                                         op=ALU.mult), deps=[ev_w2], inc=True)
        wdl = k.dta0
        ev_a4 = p.op("scalar", lambda s: s.activation(out=k.atot4[:], in_=cum[:, 127:512:128],
                                                      func=AF.Exp), deps=[ev_cs], inc=True)
        def v_split(v):
            v.tensor_copy(k.cum_hi[:], cum[:])
            v.tensor_tensor(out=k.wtmp[:], in0=cum[:], in1=k.cum_hi[:], op=ALU.subtract)
            v.tensor_copy(k.cum_lo[:], k.wtmp[:])
            v.tensor_scalar(out=k.ncum_hi[:], in0=k.cum_hi[:], scalar1=-1.0, scalar2=None,
                            op0=ALU.mult)
            return v.tensor_scalar(out=k.ncum_lo[:], in0=k.cum_lo[:], scalar1=-1.0, scalar2=None,
                                   op0=ALU.mult)
        ev_spl = p.op("vector", v_split, deps=[ev_w3, ev_Dmm_prev], inc=True)
        st["ev_spl"] = ev_spl
        st["ev_w2"] = ev_w2
        bq = BK(3, 1)
        evt = None
        for c in range(4):
            evt = p.op("tensor",
                       lambda t, c=c: t.transpose(P[3][:, 512 + c * 64:512 + (c + 1) * 64],
                                                  cum[:, c * 128:(c + 1) * 128],
                                                  k.ident_f[0:64, 0:64]),
                       deps=([ev_cs] + pb.wdeps(bq)) if c == 0 else [], inc=(c == 3))
        pb.wrote(evt, bq)
        ev_pr = p.op("vector", lambda v: v.tensor_copy(k.cum_t[:], P[3][:, 512:768]),
                     deps=pb.rdeps(bq) + [evt, ev_ect_prev], inc=True)
        pb.read(ev_pr, bq)
        ev_ect = p.op("scalar", lambda s: s.activation(out=k.expcum_t[:], in_=k.cum_t[:],
                                                       func=AF.Exp),
                      deps=[ev_pr, ev_y_evac], inc=True)
        ev_ect_prev = ev_ect
        st["ev_ect"] = ev_ect
        evt = None
        for c in range(4):
            evt = p.op("tensor",
                       lambda t, c=c: t.transpose(P[3][:, 512 + c * 64:512 + (c + 1) * 64],
                                                  k.dtf[:, c * 128:(c + 1) * 128],
                                                  k.ident_f[0:64, 0:64]),
                       deps=([ev_dtfv, ev_pr] + pb.wdeps(bq)) if c == 0 else [], inc=(c == 3))
        pb.wrote(evt, bq)
        ev_dtft = p.op("vector", lambda v: v.tensor_copy(k.dtft_bf[:], P[3][:, 512:768]),
                       deps=pb.rdeps(bq) + [evt], inc=True)
        pb.read(ev_dtft, bq)
        st["ev_dtft"] = ev_dtft
        evt = None
        for c in range(4):
            evt = p.op("tensor",
                       lambda t, c=c: t.transpose(P[3][:, 512 + c * 64:512 + (c + 1) * 64],
                                                  wdl[:, c * 128:(c + 1) * 128],
                                                  k.ident_f[0:64, 0:64]),
                       deps=([ev_w3] + pb.wdeps(bq)) if c == 0 else [], inc=(c == 3))
        pb.wrote(evt, bq)
        ev_wt = p.op("vector", lambda v: v.tensor_copy(k.wt_bf[:], P[3][:, 512:768]),
                     deps=pb.rdeps(bq) + [evt], inc=True)
        pb.read(ev_wt, bq)
        st["ev_wt"] = ev_wt
        evt = None
        for c in range(4):
            evt = p.op("tensor",
                       lambda t, c=c: t.transpose(P[3][0:1, 768 + c * 64:768 + (c + 1) * 64],
                                                  k.atot4[:, c:c + 1], k.ident_f[0:64, 0:64]),
                       deps=([ev_a4] + pb.wdeps(bq)) if c == 0 else [], inc=(c == 3))
        pb.wrote(evt, bq)
        ev_at = p.op("vector", lambda v: v.tensor_copy(k.atot_row[:], P[3][0:1, 768:1024]),
                     deps=pb.rdeps(bq) + [evt], inc=True)
        pb.read(ev_at, bq)
        evm2 = None
        for c in range(4):
            evm2 = p.op("tensor", lambda t, c=c: t.matmul(P[3][:, 512 + c * 64:512 + (c + 1) * 64],
                                                          k.ones1[0:1, :],
                                                          k.atot_row[0:1, c * 64:(c + 1) * 64],
                                                          start=True, stop=True),
                        deps=([ev_at] + pb.wdeps(bq)) if c == 0 else [], inc=(c == 3))
        pb.wrote(evm2, bq)
        ev_prep = p.op("vector", lambda v: v.tensor_copy(k.atot_bc[:], P[3][:, 512:768]),
                       deps=pb.rdeps(bq) + [evm2], inc=True)
        pb.read(ev_prep, bq)
        st["ev_prep"] = ev_prep
        if k.debug and k.stage == 3 and ti == 0:
            p.dma("sync", k.dbg_p[0:64, 0:256], k.cum_hi[:].bitcast(F32), deps=[ev_spl])
            p.dma("sync", k.dbg_p[0:64, 256:512], k.cum_lo[:].bitcast(F32), deps=[ev_spl])
            p.dma("sync", k.dbg_p[0:128, 512:640], k.dtft_bf[:].bitcast(F32), deps=[ev_dtft])
            p.dma("sync", k.dbg_p[0:128, 640:768], k.wt_bf[:].bitcast(F32), deps=[ev_wt])
            p.dma("sync", k.dbg_p[0:128, 768:1024], k.atot_bc[:], deps=[ev_prep])
            p.dma("sync", k.dbg_p[0:64, 1024:1536], k.dta1[:], deps=[ev_spl])

    def emit_chunk_start(ci):
        nonlocal ev_cbm_prev
        tcl = ci % TPC
        slzr = ci % 2
        ev_zr = p.dma("sync", k.z_rd[slzr][:], k.z_scr[ci * 128:(ci + 1) * 128, :],
                      deps=[ev_zrd_free[slzr]] + zrd_ready.get((ci // TPC, tcl), []))
        st["ev_zr"] = ev_zr
        bcb = BK(3, 1)
        evcb = p.op("tensor", lambda t, tcl=tcl: t.matmul(
            P[3][:, 512:640], k.bf_bf[:, tcl * 128:(tcl + 1) * 128],
            k.cf_bf[:, tcl * 128:(tcl + 1) * 128], start=True, stop=True),
            deps=[st["ev_bf"], st["ev_cf"], st["ev_prep"] if tcl == 0 else None]
            + pb.wdeps(bcb), inc=True)
        pb.wrote(evcb, bcb)
        evcbe = p.op("vector", lambda v: v.tensor_tensor(out=k.cbm_bf[:], in0=P[3][:, 512:640],
                                                         in1=k.tri01[:], op=ALU.mult),
                     deps=pb.rdeps(bcb) + [evcb, ev_cbm_prev,
                                           k.ev_gts if ci == 0 else None], inc=True)
        pb.read(evcbe, bcb)
        ev_cbm_prev = evcbe
        st["evcbe"] = evcbe

    def emit_group(ci, gg):
        nonlocal ev_Dmm_prev, ev_d8_free, ev_e8_free, ev_m8_free, ev_id8_free
        nonlocal ev_stbf_free, ev_wb8_free, ev_y_evac, ev_state_upd, eve8, ev_sqact_prev
        tcl = ci % TPC
        slg = gg % 2
        pD = P[1]
        bD = (BK(1, 0), BK(1, 1))
        evd = None
        for half in range(2):
            for lohi in range(2):
                lhsT = [k.ncum_hi, k.ncum_lo][lohi][:, tcl * 128:(tcl + 1) * 128]
                evd = p.op("tensor",
                           lambda t, pD=pD, half=half, gg=gg, lhsT=lhsT, lohi=lohi:
                           t.matmul(pD[:, half * 512:(half + 1) * 512], lhsT,
                                    k.sel64[:, gg * 1024 + half * 512: gg * 1024 + (half + 1) * 512],
                                    start=(lohi == 0), stop=False),
                           deps=(pb.wdeps(*bD) + [st["ev_spl"],
                                                  st["evcbe"] if gg == 0 else None])
                           if (half == 0 and lohi == 0) else [],
                           inc=False)
        for hp in range(8):
            for lohi in range(2):
                rhs = [k.cum_hi, k.cum_lo][lohi][:, tcl * 128:(tcl + 1) * 128]
                evd = p.op("tensor",
                           lambda t, pD=pD, hp=hp, gg=gg, rhs=rhs, lohi=lohi:
                           t.matmul(pD[:, hp * 128:(hp + 1) * 128],
                                    k.sel64[:, (gg * 8 + hp) * 128:(gg * 8 + hp + 1) * 128],
                                    rhs,
                                    start=False, stop=(lohi == 1 and hp in (3, 7))),
                           deps=[], inc=(hp == 7 and lohi == 1))
        pb.wrote(evd, *bD)
        ev_Dmm_prev = evd
        evdm = p.op("vector", lambda v, pD=pD, slg=slg: v.tensor_scalar(
            out=k.d8[slg][:], in0=pD[:], scalar1=0.0, scalar2=None, op0=ALU.min),
            deps=pb.rdeps(*bD) + [evd, ev_d8_free[slg],
                                  st["ev_prep"] if (tcl == 0 and gg < 2) else None],
            inc=True)
        pb.read(evdm, *bD)
        eve8 = p.op("scalar", lambda s, slg=slg: s.activation(
            out=k.e8[:], in_=k.d8[slg][:], func=AF.Exp),
            deps=[evdm, ev_e8_free], inc=True)
        ev_d8_free[slg] = eve8
        ev_sqact_prev = eve8
        evid = p.op("vector", lambda v, gg=gg: v.tensor_tensor(
            out=k.id8[:].rearrange("p (h i) -> p h i", i=128),
            in0=k.ident_bf[:].rearrange("p i -> p () i").broadcast_to([128, 8, 128]),
            in1=k.dbc[:, gg * 8:(gg + 1) * 8].rearrange("p h -> p h ()")
            .broadcast_to([128, 8, 128]),
            op=ALU.mult),
            deps=[ev_id8_free, k.ev_dbc if (ci == 0 and gg == 0) else None], inc=True)
        def v_m8(v, slg=slg, gg=gg, tcl=tcl):
            v.tensor_tensor(out=k.m8[slg][:].rearrange("p (h i) -> p h i", i=128),
                            in0=k.e8[:].rearrange("p (h i) -> p h i", i=128),
                            in1=k.cbm_bf[:].rearrange("p i -> p () i")
                            .broadcast_to([128, 8, 128]),
                            op=ALU.mult)
            v.tensor_tensor(out=k.m8[slg][:].rearrange("p (h i) -> p h i", i=128),
                            in0=k.m8[slg][:].rearrange("p (h i) -> p h i", i=128),
                            in1=k.dtft_bf[:, tcl * 64 + gg * 8: tcl * 64 + (gg + 1) * 8]
                            .rearrange("p h -> p h ()").broadcast_to([128, 8, 128]),
                            op=ALU.mult)
            return v.tensor_tensor(out=k.m8[slg][:], in0=k.m8[slg][:], in1=k.id8[:],
                                   op=ALU.add)
        evm8 = p.op("vector", v_m8,
                    deps=[eve8, evid, ev_m8_free[slg],
                          st["ev_dtft"] if gg == 0 else None],
                    inc=True)
        if k.debug and k.stage == 3 and ci == 0 and gg == 0:
            p.dma("sync", k.dbg_p[:, 1536:2048], k.m8[slg][:].bitcast(F32), deps=[evm8])
            p.dma("sync", k.dbg_p[:, 2048:2112], k.cbm_bf[:].bitcast(F32), deps=[evm8])
            p.dma("sync", k.dbg_p[:, 2112:3136], k.d8[slg][:], deps=[eve8])
            p.dma("sync", k.dbg_p[:, 3136:4096],
                  k.buf2[:, 0:1920].bitcast(F32), deps=[evm8])

        ev_e8_free = evm8
        ev_id8_free = evm8
        evsb = None
        if ci > 0:
            evsb = p.op("vector", lambda v, gg=gg, slg=slg: v.tensor_copy(
                k.st_bf[slg][:], k.state[:, gg * 512:(gg + 1) * 512]),
                deps=[ev_stbf_free[slg]], inc=True)
        evwb = p.op("vector", lambda v, slg=slg, gg=gg, tcl=tcl: v.tensor_tensor(
            out=k.wb8[slg][:].rearrange("p (h n) -> p h n", n=128),
            in0=k.btok_bf[:, tcl * 128:(tcl + 1) * 128]
            .rearrange("p n -> p () n").broadcast_to([128, 8, 128]),
            in1=k.wt_bf[:, tcl * 64 + gg * 8: tcl * 64 + (gg + 1) * 8]
            .rearrange("p h -> p h ()").broadcast_to([128, 8, 128]),
            op=ALU.mult),
            deps=[ev_wb8_free[slg], st.get("ev_btok") if gg == 0 else None,
                  st["ev_wt"] if gg == 0 else None], inc=True)
        bYd, bYo, bst_ = BK(2, 0), BK(2, 1), BK(3, 0)
        evyd = None
        for hp in range(8):
            h = gg * 8 + hp
            evyd = p.op("tensor",
                        lambda t, hp=hp, h=h, slg=slg, tcl=tcl:
                        t.matmul(P[2][:, hp * 64:(hp + 1) * 64],
                                 k.m8[slg][:, hp * 128:(hp + 1) * 128],
                                 k.buf2[:, tcl * 4096 + h * 64: tcl * 4096 + (h + 1) * 64],
                                 start=True, stop=True),
                        deps=([evm8] + pb.wdeps(bYd)) if hp == 0 else [], inc=(hp == 7))
        pb.wrote(evyd, bYd)
        ev_m8_free[slg] = evyd
        evyo = None
        if ci > 0:
            for hp in range(8):
                evyo = p.op("tensor",
                            lambda t, hp=hp, slg=slg, tcl=tcl:
                            t.matmul(P[2][:, 512 + hp * 64: 512 + (hp + 1) * 64],
                                     k.cf_bf[:, tcl * 128:(tcl + 1) * 128],
                                     k.st_bf[slg][:, hp * 64:(hp + 1) * 64],
                                     start=True, stop=True),
                            deps=([evsb] + pb.wdeps(bYo)) if hp == 0 else [],
                            inc=(hp == 7))
            pb.wrote(evyo, bYo)
            ev_stbf_free[slg] = evyo
        evst = None
        for hp in range(8):
            h = gg * 8 + hp
            evst = p.op("tensor",
                        lambda t, hp=hp, h=h, slg=slg, tcl=tcl:
                        t.matmul(P[3][:, hp * 64:(hp + 1) * 64],
                                 k.wb8[slg][:, hp * 128:(hp + 1) * 128],
                                 k.buf2[:, tcl * 4096 + h * 64: tcl * 4096 + (h + 1) * 64],
                                 start=True, stop=True),
                        deps=([evwb] + pb.wdeps(bst_)) if hp == 0 else [], inc=(hp == 7))
        pb.wrote(evst, bst_)
        ev_wb8_free[slg] = evst
        st["evst"] = evst
        if ci > 0:
            def v_ye(v, gg=gg, tcl=tcl):
                v.tensor_tensor(out=k.y_bf[:, gg * 512:(gg + 1) * 512]
                                .rearrange("p (h q) -> p h q", q=64),
                                in0=P[2][:, 512:1024].rearrange("p (h q) -> p h q", q=64),
                                in1=k.expcum_t[:, tcl * 64 + gg * 8: tcl * 64 + (gg + 1) * 8]
                                .rearrange("p h -> p h ()").broadcast_to([128, 8, 64]),
                                op=ALU.mult)
                return v.tensor_tensor(out=k.y_bf[:, gg * 512:(gg + 1) * 512],
                                       in0=k.y_bf[:, gg * 512:(gg + 1) * 512],
                                       in1=P[2][:, 0:512], op=ALU.add)
            ev_y_evac = p.op("vector", v_ye,
                             deps=pb.rdeps(bYd, bYo) + [evyd, evyo,
                                                        st["ev_ect"] if gg == 0 else None,
                                                        ev_norm_prev if gg == 0 else None],
                             inc=True)
            pb.read(ev_y_evac, bYd, bYo)
        else:
            ev_y_evac = p.op("vector", lambda v, gg=gg: v.tensor_copy(
                k.y_bf[:, gg * 512:(gg + 1) * 512], P[2][:, 0:512]),
                deps=pb.rdeps(bYd) + [evyd, ev_norm_prev if gg == 0 else None], inc=True)
            pb.read(ev_y_evac, bYd)
        def v_su(v, gg=gg, tcl=tcl):
            v.tensor_tensor(out=k.state[:, gg * 512:(gg + 1) * 512]
                            .rearrange("p (h q) -> p h q", q=64),
                            in0=k.state[:, gg * 512:(gg + 1) * 512]
                            .rearrange("p (h q) -> p h q", q=64),
                            in1=k.atot_bc[:, tcl * 64 + gg * 8: tcl * 64 + (gg + 1) * 8]
                            .rearrange("p h -> p h ()").broadcast_to([128, 8, 64]),
                            op=ALU.mult)
            return v.tensor_tensor(out=k.state[:, gg * 512:(gg + 1) * 512],
                                   in0=k.state[:, gg * 512:(gg + 1) * 512],
                                   in1=P[3][:, 0:512], op=ALU.add)
        if k.debug and k.stage == 3 and ci == 0 and gg == 0:
            p.dma("sync", k.dbg_p[0:128, 1024:1280], k.y_bf[:, 0:512].bitcast(F32),
                  deps=[ev_y_evac])
        ev_state_upd[gg] = p.op("vector", v_su,
                                deps=pb.rdeps(bst_) + [evst,
                                                       st["ev_prep"] if gg == 0 else None],
                                inc=True)
        pb.read(ev_state_upd[gg], bst_)
        if ci == 0:
            ev_stbf_free[slg] = ev_state_upd[gg]

    def emit_chunk_end(ci):
        nonlocal ev_zrd_free, ev_norm_prev, ev_sqact_prev, ev_ssq, ev_tT, evt_
        tcl = ci % TPC
        slzr = ci % 2
        evth = p.op("scalar", lambda s, slzr=slzr: s.activation(
            out=k.th_bf[:], in_=k.z_rd[slzr][:], func=AF.Tanh, scale=0.5),
            deps=[st["ev_zr"], ev_norm_prev], inc=True)
        def v_norm(v, slzr=slzr):
            v.tensor_scalar(out=k.th_bf[:], in0=k.th_bf[:], scalar1=0.5, scalar2=0.5,
                            op0=ALU.mult, op1=ALU.add)
            v.tensor_tensor(out=k.th_bf[:], in0=k.th_bf[:], in1=k.z_rd[slzr][:], op=ALU.mult)
            return v.tensor_tensor(out=k.t_bf[:], in0=k.th_bf[:], in1=k.y_bf[:], op=ALU.mult)
        evt_ = p.op("vector", v_norm, deps=[evth, ev_y_evac, ev_sqact_prev], inc=True)
        ev_zrd_free[slzr] = evt_
        evsq = p.op("scalar", lambda s: s.activation(out=k.th_bf[:], in_=k.t_bf[:],
                                                     func=AF.Square), deps=[evt_], inc=True)
        ev_sqact_prev = evsq
        ev_ssq = p.op("vector", lambda v, tcl=tcl: v.tensor_reduce(
            out=k.ssq_t[:, tcl:tcl + 1], in_=k.th_bf[:], axis=AX.X, op=ALU.add),
            deps=[evsq, ev_sq2_prev if tcl == 0 else None], inc=True)
        ev_norm_prev = ev_ssq
        st["ev_ssq"] = ev_ssq
        evtT = None
        for qq in range(8):
            slq = qq % 2
            btq = BK(2, slq)
            ptT = [P[2][:, 0:256].bitcast(BF16), P[2][:, 512:768].bitcast(BF16)][slq]
            evtr = None
            for j in range(4):
                kk = qq * 4 + j
                evtr = p.op("tensor",
                            lambda t, ptT=ptT, j=j, kk=kk: t.transpose(
                                ptT[:, j * 128:(j + 1) * 128],
                                k.t_bf[:, kk * 128:(kk + 1) * 128], k.ident_bf[:]),
                            deps=([evt_, st["evst"]] + pb.wdeps(btq)) if j == 0 else [],
                            inc=(j == 3))
            pb.wrote(evtr, btq)
            for j in range(4):
                kk = qq * 4 + j
                evtT = p.op("vector",
                            lambda v, ptT=ptT, j=j, kk=kk, tcl=tcl: v.tensor_scalar(
                                out=k.buf2[:, tcl * 4096 + kk * 128: tcl * 4096 + (kk + 1) * 128],
                                in0=ptT[:, j * 128:(j + 1) * 128],
                                scalar1=k.nw_col[:, kk:kk + 1], scalar2=None, op0=ALU.mult),
                            deps=[evtr, k.ev_nw if (ci == 0 and kk == 0) else None]
                            if j == 0 else [],
                            inc=(j == 3))
            pb.read(evtT, btq)
        ev_tT[tcl] = evtT

    def emit_rstd_wout(ti):
        nonlocal ev_sq2_prev, ev_wob_free, ev_attst_free, ev_wout_done, ev_wibdt_free, evat
        p.op("scalar", lambda s: s.preload_act_table(AF.Sqrt))
        ev_sq2 = p.op("scalar", lambda s: s.activation(out=k.rstd_t[:], in_=k.ssq_t[:, 0:4],
                                                       func=AF.Sqrt, bias=k.eps_col[:],
                                                       scale=1.0 / DIN),
                      deps=[st["ev_ssq"]], inc=True)
        ev_sq2_prev = ev_sq2
        ev_rc = p.op("vector", lambda v: v.reciprocal(k.rstd_t[:], k.rstd_t[:]),
                     deps=[ev_sq2], inc=True)
        bq = BK(3, 1)
        evt = None
        for c in range(4):
            evt = p.op("tensor", lambda t, c=c: t.transpose(
                P[3][0:1, 512 + c * 128: 512 + c * 128 + 128], k.rstd_t[:, c:c + 1], k.ident_f[:]),
                deps=([ev_rc] + pb.wdeps(bq)) if c == 0 else [], inc=(c == 3))
        pb.wrote(evt, bq)
        ev_rr = p.op("vector", lambda v: v.tensor_copy(k.rstd_row[0:1, :], P[3][0:1, 512:1024]),
                     deps=pb.rdeps(bq) + [evt], inc=True)
        pb.read(ev_rr, bq)
        bbc = BK(2, 0)
        evbc = p.op("tensor", lambda t: t.matmul(P[2][:, 0:512], k.ones1[0:1, :],
                                                 k.rstd_row[0:1, :], start=True, stop=True),
                    deps=[ev_rr] + pb.wdeps(bbc), inc=True)
        pb.wrote(evbc, bbc)
        ev_rstdbc = p.op("vector", lambda v: v.tensor_copy(k.rstd_bc[:, 0:512], P[2][:, 0:512]),
                         deps=pb.rdeps(bbc) + [evbc, ev_wib_free[0], ev_wibdt_free], inc=True)
        pb.read(ev_rstdbc, bbc)
        for m in range(16):
            slw = m % 2
            p.dma("gpsimd", k.wob[slw].rearrange("p (kk dd) -> p kk dd", dd=128)[:, 0:16, :],
                  k.w_out.rearrange("(kk p) dd -> p kk dd", p=128)[:, 0:16, m * 128:(m + 1) * 128],
                  deps=[ev_wob_free[slw], k.ev_gts if (ti == 0 and m < 2) else None,
                        (st.get("ev_spl") if m < 2 else None),
                        (st.get("ev_dtft") if m < 2 else None),
                        (st.get("ev_w2") if m < 2 else None),
                        (st.get("ev_wt") if m < 2 else None)]
                  + ([z for z in zdma if z is not None] if m < 2 else []))
            evw2 = p.dma("gpsimd", k.wob[slw].rearrange("p (kk dd) -> p kk dd", dd=128)[:, 16:32, :],
                         k.w_out.rearrange("(kk p) dd -> p kk dd", p=128)[:, 16:32, m * 128:(m + 1) * 128],
                         deps=[])
            slp = m % 2
            bam = BK(0, slp)
            pam = P[0][:, slp * 512:(slp + 1) * 512].rearrange("p (c t) -> p c t", t=128)
            evm = None
            for kk in range(32):
                rhs = k.buf2[:].rearrange("p (c f) -> p c f", f=4096)[:, :, kk * 128:(kk + 1) * 128]
                evm = p.op("tensor",
                           lambda t, pam=pam, kk=kk, slw=slw, rhs=rhs:
                           t.matmul(pam, k.wob[slw][:, kk * 128:(kk + 1) * 128], rhs,
                                    start=(kk == 0), stop=(kk == 31)),
                           deps=([evw2, ev_tT[0], ev_tT[1], ev_tT[2], ev_tT[3]]
                                 + pb.wdeps(bam)) if kk == 0 else [],
                           inc=(kk == 31))
            pb.wrote(evm, bam)
            ev_wob_free[slw] = evm
            if m == 15:
                ev_wout_done = evm
            evat = p.op("vector", lambda v, slp=slp, slw=slw: v.tensor_tensor(
                out=k.att_st[slw][:], in0=P[0][:, slp * 512:(slp + 1) * 512],
                in1=k.rstd_bc[:, 0:512], op=ALU.mult),
                deps=pb.rdeps(bam) + [evm, ev_rstdbc, ev_attst_free[slw]], inc=True)
            pb.read(evat, bam)
            evad = p.dma("sync", k.att_scr[m * 128:(m + 1) * 128, ti * 512:(ti + 1) * 512],
                         k.att_st[slw][:], deps=[evat])
            ev_attst_free[slw] = evad
            k.ev_att_w.append(evad)
        ev_wibdt_free = evat

    for step in range(NT + 1):
        if step < NT:
            ev_xnt = p.dma("sync", k.xn_tile.rearrange("p (kk t) -> p kk t", t=512),
                           k.xn_scr.rearrange("(kk p) t -> p kk t", p=128)[:, :, step * 512:(step + 1) * 512],
                           deps=[ev_xnt_free] + k.ev_xn_dma[step * 4:step * 4 + 4])
            st["ev_xnt"] = ev_xnt
        if step == 0:
            for mg in range(16):
                emit_win_m(0, mg, chunkloop=False)
        else:
            ti_c = step - 1
            for tcl in range(TPC):
                ci = ti_c * TPC + tcl
                emit_chunk_start(ci)
                for gg in range(8):
                    emit_group(ci, gg)
                    if step < NT and gg % 2 == 1:
                        emit_win_m(step, (tcl * 4 + gg // 2), chunkloop=True)
                if k.debug and k.stage == 3 and ci in k.dbg_ci:
                    ixd = k.dbg_ci.index(ci)
                    p.dma("sync", k.dbg_y[ixd * 128:(ixd + 1) * 128, 0:2048],
                          k.y_bf[:, 0:4096].bitcast(F32), deps=[ev_y_evac])
                    p.dma("sync", k.dbg_st[ixd * 128:(ixd + 1) * 128, :], k.state[:],
                          deps=[ev_state_upd[7]])
                emit_chunk_end(ci)
            emit_rstd_wout(ti_c)
        if step < NT:
            p.op("scalar", lambda s: s.preload_act_table(AF.Silu))
            for mg in range(16, 34):
                emit_win_m(step, mg, chunkloop=False)
            bt = BK(2, 0)
            evbt = None
            for tcl in range(4):
                evbt = p.op("tensor",
                            lambda t, tcl=tcl: t.transpose(
                                P[2][:, tcl * 64:(tcl + 1) * 64].bitcast(BF16),
                                k.bf_bf[:, tcl * 128:(tcl + 1) * 128], k.ident_bf[:]),
                            deps=([ev_bf] + pb.wdeps(bt)) if tcl == 0 else [], inc=(tcl == 3))
            pb.wrote(evbt, bt)
            ev_btok = p.op("vector", lambda v: v.tensor_copy(k.btok_bf[:],
                                                             P[2][:, 0:256].bitcast(BF16)),
                           deps=pb.rdeps(bt) + [evbt], inc=True)
            pb.read(ev_btok, bt)
            st["ev_btok"] = ev_btok
            if k.debug and k.stage == 2 and step == 0:
                p.dma("sync", k.dbg_xtok, k.buf2[:], deps=[eve, ev_btok])
            emit_prep(step)
            ev_rh_free = [ev_wob_free[0], ev_wob_free[1], st.get("ev_spl"),
                          st.get("ev_dtft"), st.get("ev_wt"), st.get("ev_w2")]

    k.ev_wout_done = ev_wout_done
    k.ev_ssm_last_dv = ev_tT[3]
    k.ev_Dmm_last = ev_Dmm_prev
    k.ev_attst_free = ev_attst_free
    k.ev_arz_free2 = [evt_, ev_ssq, eve8, ev_tT[3]]
    k.ev_zrd_free = list(ev_zrd_free)
    k.ev_wib_free = list(ev_wib_free)
    k.ev_xnt_free = ev_xnt_free
    k.ev_zstage_free = [e for e in zdma if e is not None]
    k.ev_spl_last = ev_spl if ev_spl is not None else st.get("ev_spl")
    k.ev_wt_last = ev_wt if ev_wt is not None else st.get("ev_wt")
    k.ev_rstd_last = ev_wibdt_free


def phase3(k):
    p, pb, P = k.p, k.pb, k.P
    ev_eb = p.dma("gpsimd", k.eb_bf[:], k.exp_b,
                  deps=[k.ev_attst_free[0], k.ev_attst_free[1], k.ev_spl_last, k.ev_wt_last])
    ev_web_free = [k.ev_zrd_free[0], k.ev_zrd_free[1]]
    ev_rhs_free = [k.ev_Dmm_last] * 8
    ev_attrd_free = [None, None]
    ev_outst_free = [None, None]
    ev_u2_free = [None, None]
    ev_outf_free = k.ev_wout_done
    ev_xng_dep = [k.ev_wib_free[0], k.ev_wib_free[1], k.ev_xnt_free, k.ev_rstd_last,
                  k.ev_xn_dma[-1]]
    ev_g8_dep = list(k.ev_arz_free2)
    ev_asm_last = None
    for tg in range(NTG):
        ev_xng = p.dma("sync", k.xn_tg.rearrange("p (kk t) -> p kk t", t=TG),
                       k.xn_scr.rearrange("(kk p) t -> p kk t", p=128)[:, :, tg * TG:(tg + 1) * TG],
                       deps=ev_xng_dep)
        ev_g8 = None
        for e in range(NE):
            bk = (BK(0, 0), BK(0, 1))
            evm = None
            for half in range(2):
                evm = p.op("tensor",
                           lambda t, e=e, half=half, tg=tg: t.matmul(
                               P[0][:, half * 512:(half + 1) * 512], k.sel8[:, e * 128:(e + 1) * 128],
                               k.gates_bf[:, tg * 1024 + half * 512: tg * 1024 + (half + 1) * 512],
                               start=True, stop=True),
                           deps=([k.ev_gts] + pb.wdeps(*bk)) if half == 0 else [], inc=(half == 1))
            pb.wrote(evm, *bk)
            ev_g8 = p.op("vector", lambda v, e=e: v.tensor_copy(
                k.g8_bf[:, e * 1024:(e + 1) * 1024], P[0][:]),
                deps=pb.rdeps(*bk) + [evm] + (ev_g8_dep if (tg == 0 and e == 0) else []),
                inc=True)
            pb.read(ev_g8, *bk)
        ev_g8_dep = []
        ev_evac_last = None
        for mg in range(8):
            pair = (0, 1) if mg % 2 == 0 else (2, 3)
            bksall = (BK(pair[0], 0), BK(pair[0], 1), BK(pair[1], 0), BK(pair[1], 1))
            evkk = None
            for e in range(NE):
                sle = e % 2
                evwd = p.dma("gpsimd", k.web[sle].rearrange("p (kk m) -> p kk m", m=256),
                             k.exp_w[e].rearrange("(kk p) m -> p kk m", p=128)[:, :, mg * 256:(mg + 1) * 256],
                             deps=[ev_web_free[sle]])
                for kk in range(16):
                    slr = kk % 8
                    evrh = p.op("vector",
                                lambda v, kk=kk, e=e, slr=slr: v.tensor_tensor(
                                    out=k.rhs_ek[slr][:], in0=k.xn_tg[:, kk * 1024:(kk + 1) * 1024],
                                    in1=k.g8_bf[:, e * 1024:(e + 1) * 1024], op=ALU.mult),
                                deps=[ev_rhs_free[slr], ev_g8 if (kk == 0 and e == 0) else None,
                                      ev_xng if (kk == 0 and e == 0) else None],
                                inc=True)
                    ev_rhs_free[slr] = None
                    first = (e == 0 and kk == 0)
                    evmm = None
                    for mi in range(2):
                        for half in range(2):
                            evmm = p.op("tensor",
                                        lambda t, pair=pair, mi=mi, half=half, kk=kk, sle=sle,
                                        slr=slr, e=e:
                                        t.matmul(P[pair[mi]][:, half * 512:(half + 1) * 512],
                                                 k.web[sle][:, kk * 256 + mi * 128: kk * 256 + mi * 128 + 128],
                                                 k.rhs_ek[slr][:, half * 512:(half + 1) * 512],
                                                 start=(e == 0 and kk == 0), stop=False),
                                        deps=([evrh, evwd] + (pb.wdeps(*bksall) if first else []))
                                        if (mi == 0 and half == 0) else [],
                                        inc=(mi == 1 and half == 1 and kk % 4 == 3))
                    if kk % 4 == 3:
                        evkk = evmm
                        for back in range(4):
                            ev_rhs_free[(kk - back) % 8] = evmm
                ev_web_free[sle] = evkk
            evb = None
            for mi in range(2):
                m = mg * 2 + mi
                for half in range(2):
                    evb = p.op("tensor",
                               lambda t, pair=pair, mi=mi, half=half, m=m, tg=tg:
                               t.matmul(P[pair[mi]][:, half * 512:(half + 1) * 512],
                                        k.eb_bf[:, m * 128:(m + 1) * 128],
                                        k.gates_bf[:, tg * 1024 + half * 512: tg * 1024 + (half + 1) * 512],
                                        start=False, stop=True),
                               deps=[ev_eb if (tg == 0 and mg == 0 and mi == 0 and half == 0) else None],
                               inc=(mi == 1 and half == 1))
            pb.wrote(evb, *bksall)
            for mi in range(2):
                m = mg * 2 + mi
                sla = m % 2
                evar = p.dma("sync", k.att_rd[sla][:],
                             k.att_scr[m * 128:(m + 1) * 128, tg * 1024:(tg + 1) * 1024],
                             deps=[ev_attrd_free[sla],
                                   k.ev_att_w[2 * tg * 16 + m],
                                   k.ev_att_w[(2 * tg + 1) * 16 + m]]
                             + (k.ev_arz_free2 if (tg == 0 and m < 2) else []))
                eve = p.op("vector",
                           lambda v, pair=pair, mi=mi, m=m, sla=sla: v.tensor_tensor(
                               out=k.outf[:, m * 1024:(m + 1) * 1024], in0=P[pair[mi]][:],
                               in1=k.att_rd[sla][:], op=ALU.add),
                           deps=pb.rdeps(BK(pair[mi], 0), BK(pair[mi], 1)) +
                           [evb, evar,
                            ev_outf_free if (tg == 0 and mg < 2 and mi == 0) else None],
                           inc=True)
                pb.read(eve, BK(pair[mi], 0), BK(pair[mi], 1))
                ev_attrd_free[sla] = eve
                ev_evac_last = eve
        ev_outf_free = None
        for tcl in range(8):
            tok0 = tg * 1024 + tcl * 128
            slu = tcl % 2
            ev_u2 = p.dma("sync", k.u_sb[slu][:], k.u[tok0:tok0 + 128, :],
                          deps=[ev_u2_free[slu],
                                (k.ev_ssm_last_dv if (tg == 0 and tcl < 2) else None)])
            ev_q = []
            for q in range(4):
                slq = q % 2
                bk = BK(slq, 0)
                pT = P[slq][:, 0:256].bitcast(BF16)
                evt = None
                for j in range(4):
                    m = q * 4 + j
                    evt = p.op("tensor",
                               lambda t, pT=pT, j=j, m=m, tcl=tcl: t.transpose(
                                   pT[:, j * 128:(j + 1) * 128],
                                   k.outf[:, m * 1024 + tcl * 128: m * 1024 + (tcl + 1) * 128],
                                   k.ident_bf[:]),
                               deps=([ev_evac_last] + pb.wdeps(bk)) if j == 0 else [],
                               inc=(j == 3))
                pb.wrote(evt, bk)
                eve = p.op("vector",
                           lambda v, pT=pT, q=q, slu=slu: v.tensor_tensor(
                               out=k.out_st[slu][:, q * 512:(q + 1) * 512], in0=pT,
                               in1=k.u_sb[slu][:, q * 512:(q + 1) * 512], op=ALU.add),
                           deps=pb.rdeps(bk) + [evt, ev_u2 if q == 0 else None,
                                                ev_outst_free[slu] if q == 0 else None],
                           inc=True)
                pb.read(eve, bk)
                ev_q.append(eve)
            evod = p.dma("sync", k.out[tok0:tok0 + 128, :], k.out_st[slu][:], deps=ev_q)
            ev_outst_free[slu] = evod
            ev_u2_free[slu] = ev_q[-1]
            ev_asm_last = ev_q[-1]
        ev_xng_dep = [ev_asm_last]


from concourse.bass_utils import run_bass_kernel_spmd


def kernel(L0_attn_u, L0_attn_W_in, L0_attn_conv_w, L0_attn_conv_b, L0_attn_dt_bias,
           L0_attn_A_log, L0_attn_D, L0_attn_norm_w, L0_attn_W_out, L0_ffn_gate_w,
           L0_ffn_expert_w, L0_ffn_expert_b):
    common = dict(
        w_in=np.ascontiguousarray(np.asarray(L0_attn_W_in, dtype=np.float32)),
        conv_w=np.ascontiguousarray(np.asarray(L0_attn_conv_w, dtype=np.float32)),
        conv_b=np.ascontiguousarray(np.asarray(L0_attn_conv_b, dtype=np.float32)),
        dt_bias=np.ascontiguousarray(np.asarray(L0_attn_dt_bias, dtype=np.float32)),
        a_log=np.ascontiguousarray(np.asarray(L0_attn_A_log, dtype=np.float32)),
        dvec=np.ascontiguousarray(np.asarray(L0_attn_D, dtype=np.float32)),
        norm_w=np.ascontiguousarray(np.asarray(L0_attn_norm_w, dtype=np.float32)),
        w_out=np.ascontiguousarray(np.asarray(L0_attn_W_out, dtype=np.float32)),
        gate_w=np.ascontiguousarray(np.asarray(L0_ffn_gate_w, dtype=np.float32)),
        exp_w=np.ascontiguousarray(np.asarray(L0_ffn_expert_w, dtype=np.float32)),
        exp_b=np.ascontiguousarray(np.asarray(L0_ffn_expert_b, dtype=np.float32)),
    )
    uu = np.ascontiguousarray(np.asarray(L0_attn_u, dtype=np.float32))
    in_maps = [dict(u=np.ascontiguousarray(uu[c]), **common) for c in range(8)]
    nc = build_nc(debug=0, stage=9)
    res = run_bass_kernel_spmd(nc, in_maps, core_ids=list(range(8)))
    return np.stack([np.asarray(res.results[c]["out"], dtype=np.float32) for c in range(8)], axis=0)


# version 04
# speedup vs baseline: 1.0367x; 1.0367x over previous
import numpy as np
from contextlib import ExitStack
import concourse.bass as bass
import concourse.mybir as mybir

F32 = mybir.dt.float32
BF16 = mybir.dt.bfloat16
AF = mybir.ActivationFunctionType
ALU = mybir.AluOpType
AX = mybir.AxisListType

L, D, DIN = 2048, 2048, 4096
H, PDIM, NS, CHK = 64, 64, 128, 128
DCC, NCC, DPROJ = 4352, 34, 8512
TILE, NT, TPC = 512, 4, 4
NE, EPS = 8, 1e-5
TG, NTG = 1024, 2
INTERLEAVE = False

ENGS = ("tensor", "vector", "scalar", "gpsimd", "sync")


class Sched:
    def __init__(self, nc, ctx):
        names = ["PE", "DV", "AC", "GP", "WG", "SC"]
        self.sem = {n: ctx.enter_context(nc.semaphore(n)) for n in names}
        self.cnt = {n: 0 for n in names}
        self.streams = {e: [] for e in ENGS}
        self.engsem = {"tensor": "PE", "vector": "DV", "scalar": "AC",
                       "gpsimd": "GP", "sync": None}

    def _flat(self, deps):
        dl = []
        for d in deps:
            if d is None:
                continue
            if isinstance(d, list):
                dl += [x for x in d if x is not None]
            else:
                dl.append(d)
        return tuple(dl)

    def op(self, eng, fn, deps=(), inc=False):
        dl = self._flat(deps)
        if inc:
            semn = self.engsem[eng]
            self.cnt[semn] += 1
            self.streams[eng].append((dl, fn, semn, 1))
            return (semn, self.cnt[semn])
        self.streams[eng].append((dl, fn, None, 1))
        return None

    def dma(self, eng, out, in_, deps=(), slow=False):
        s = "WG" if eng == "gpsimd" else "SC"
        def fn(e, out=out, in_=in_, slow=slow):
            return e.dma_start(out=out, in_=in_, allow_slow_non_contiguous=slow)
        self.cnt[s] += 16
        self.streams[eng].append((self._flat(deps), fn, s, 16))
        return (s, self.cnt[s])

    def wait(self, eng, deps):
        self.op(eng, lambda e: None, deps=deps)

    def replay(self, block):
        def make(engname):
            maxw = {}
            own = self.engsem[engname]
            stream = self.streams[engname]
            sems = self.sem

            def body(eobj):
                for dl, fn, isem, incn in stream:
                    for (sn, v) in dl:
                        if sn == own:
                            continue
                        if maxw.get(sn, 0) >= v:
                            continue
                        eobj.wait_ge(sems[sn], v)
                        maxw[sn] = v
                    instr = fn(eobj)
                    if isem is not None:
                        assert instr is not None, f"inc on None instr ({engname})"
                        instr.then_inc(sems[isem], incn)
            return body
        block.tensor(make("tensor"))
        block.vector(make("vector"))
        block.scalar(make("scalar"))
        block.gpsimd(make("gpsimd"))
        block.sync(make("sync"))


class Banks:
    def __init__(self):
        self.w = {}
        self.r = {}

    def wdeps(self, *banks):
        out = []
        for b in banks:
            out += list(self.r.get(b, {}).values())
        return out

    def wrote(self, ev, *banks):
        for b in banks:
            self.w[b] = ev
            self.r[b] = {}

    def rdeps(self, *banks):
        return [self.w.get(b) for b in banks]

    def read(self, ev, *banks):
        for b in banks:
            self.r.setdefault(b, {})[ev[0]] = ev


class K:
    pass


BK = lambda i, h: (i, h)


def SP(k, v):
    v.tensor_copy(k.spc[:], k.ident_f[:])


def build_nc(debug=0, stage=9, dbg_ci=(0, 5), races=True):
    nc = bass.Bass("TRN2", target_bir_lowering=False, debug=False,
                   detect_race_conditions=races)
    k = K()
    k.debug, k.stage, k.dbg_ci = debug, stage, tuple(dbg_ci)
    k.u = nc.dram_tensor("u", [L, D], F32, kind="ExternalInput").ap()
    k.w_in = nc.dram_tensor("w_in", [D, DPROJ], F32, kind="ExternalInput").ap()
    k.conv_w = nc.dram_tensor("conv_w", [DCC, 4], F32, kind="ExternalInput").ap()
    k.conv_b = nc.dram_tensor("conv_b", [DCC], F32, kind="ExternalInput").ap()
    k.dt_bias = nc.dram_tensor("dt_bias", [H], F32, kind="ExternalInput").ap()
    k.a_log = nc.dram_tensor("a_log", [H], F32, kind="ExternalInput").ap()
    k.dvec = nc.dram_tensor("dvec", [H], F32, kind="ExternalInput").ap()
    k.norm_w = nc.dram_tensor("norm_w", [DIN], F32, kind="ExternalInput").ap()
    k.w_out = nc.dram_tensor("w_out", [DIN, D], F32, kind="ExternalInput").ap()
    k.gate_w = nc.dram_tensor("gate_w", [NE, D], F32, kind="ExternalInput").ap()
    k.exp_w = nc.dram_tensor("exp_w", [NE, D, D], F32, kind="ExternalInput").ap()
    k.exp_b = nc.dram_tensor("exp_b", [NE, D], F32, kind="ExternalInput").ap()
    k.out = nc.dram_tensor("out", [L, D], F32, kind="ExternalOutput").ap()
    scrkind = "ExternalOutput" if debug else "Internal"
    k.xn_scr = nc.dram_tensor("xn_scr", [D, L // 2], F32, kind=scrkind).ap().bitcast(BF16)
    k.z_scr = nc.dram_tensor("z_scr", [L, DIN // 2], F32, kind=scrkind).ap().bitcast(BF16)
    k.att_scr = nc.dram_tensor("att_scr", [D, L], F32, kind=scrkind).ap()
    if debug:
        k.dbg_g = nc.dram_tensor("dbg_g", [NE, L], F32, kind="ExternalOutput").ap()
        k.dbg_xtok = nc.dram_tensor("dbg_xtok", [128, 8192], F32, kind="ExternalOutput").ap().bitcast(BF16)
        k.dbg_y = nc.dram_tensor("dbg_y", [2 * 128, DIN], F32, kind="ExternalOutput").ap()
        k.dbg_p = nc.dram_tensor("dbg_p", [128, 4096], F32, kind="ExternalOutput").ap()
        k.dbg_st = nc.dram_tensor("dbg_st", [2 * 128, DIN], F32, kind="ExternalOutput").ap()

    ctx = ExitStack()
    with ctx:
        sb = lambda name, shp, dt: ctx.enter_context(nc.sbuf_tensor(name, shp, dt))
        k.P = [ctx.enter_context(nc.psum_tensor(f"P{i}", [128, 1024], F32)) for i in range(4)]
        k.pb = Banks()

        arz = sb("arz", [128, 16384], BF16)
        ar1 = sb("ar1", [128, 16384], BF16)
        ar2b = sb("ar2b", [128, 8192], BF16)
        ar4 = sb("ar4", [128, 8192], BF16)
        ar5 = sb("ar5", [128, 4096], F32)
        ar6 = sb("ar6", [128, 8192], BF16)
        ar7 = sb("ar7", [128, 1024], F32)
        ar8 = sb("ar8", [128, 2048], BF16)
        k.buf2 = sb("buf2", [128, 16384], BF16)
        k.state = sb("state", [128, DIN], F32)

        k.z_stage = [arz[:, i * 4096:(i + 1) * 4096] for i in range(4)]
        k.y_bf = arz[:, 0:4096]
        k.th_bf = arz[:, 4096:8192]
        k.d8 = [arz[:, 8192 + i * 2048: 8192 + (i + 1) * 2048].bitcast(F32) for i in range(2)]
        k.g8_bf = arz[:, 0:8192]
        k.att_rd = [arz[:, 8192 + i * 2048: 8192 + (i + 1) * 2048].bitcast(F32) for i in range(2)]
        k.xn_tile = ar1[:, 0:8192]
        k.wib = [ar1[:, 8192 + i * 4096: 8192 + (i + 1) * 4096] for i in range(2)]
        k.xn_tg = ar1[:, :]
        k.sel64 = ar2b[0:64, :]
        k.sel128 = ar2b[:, :]
        k.rhs_ek = [ar2b[:, i * 1024:(i + 1) * 1024] for i in range(8)]
        k.z_rd = [ar4[:, i * 4096:(i + 1) * 4096] for i in range(2)]
        k.web = [ar4[:, i * 4096:(i + 1) * 4096] for i in range(2)]
        k.u_sb = [ar5[:, i * 2048:(i + 1) * 2048] for i in range(2)]
        k.st_bf = [ar5[:, i * 256:(i + 1) * 256].bitcast(BF16) for i in range(2)]
        k.m8 = [ar5[:, 512 + i * 512: 512 + (i + 1) * 512].bitcast(BF16) for i in range(2)]
        k.wb8 = [ar5[:, 1536 + i * 512: 1536 + (i + 1) * 512].bitcast(BF16) for i in range(2)]
        k.e8 = ar5[:, 2560:3072].bitcast(BF16)
        k.id8 = ar5[:, 3072:3584].bitcast(BF16)
        k.cbm_bf = ar5[:, 3584:3648].bitcast(BF16)
        k.usq = ar6[:, 0:4096].bitcast(F32)
        k.xn_tok = ar6[:, 4096:6144]
        k.xn_stage = ar6[:, 6144:8192]
        k.wob = [ar6[:, i * 4096:(i + 1) * 4096] for i in range(2)]
        k.att_st = [ar7[:, i * 512:(i + 1) * 512] for i in range(2)]
        k.eb_bf = ar7[0:8, :].bitcast(BF16)
        k.gw_bf = ar8[0:8, :]
        k.zf_bf = [ar8[:, i * 512:(i + 1) * 512] for i in range(2)]
        k.xf_bf = [ar8[:, 1024 + i * 512: 1024 + (i + 1) * 512] for i in range(2)]
        k.out_st = [k.state[:, i * 2048:(i + 1) * 2048] for i in range(2)]
        k.outf = k.buf2[:, :]
        k.zst = [k.wob[i // 2][:, (i % 2) * 2048:(i % 2 + 1) * 2048] for i in range(4)]
        k.pzT = None
        k.t_bf = k.y_bf

        k.ones1 = sb("ones1", [1, 128], F32)
        k.one64 = sb("one64", [64, 1], F32)
        k.ident_bf = sb("ident_bf", [128, 128], BF16)
        k.ident_f = sb("ident_f", [128, 128], F32)
        k.tri01 = sb("tri01", [128, 128], BF16)
        k.sel8 = sb("sel8", [8, 1024], BF16)
        k.eps_col = sb("eps_col", [128, 1], F32)
        k.cwcol = sb("cwcol", [128, 136], F32)
        k.cb_col = sb("cb_col", [128, 34], F32)
        k.dtb_col = sb("dtb_col", [64, 1], F32)
        k.a_col = sb("a_col", [64, 1], F32)
        k.d_row = sb("d_row", [1, 64], F32)
        k.dbc = sb("dbc", [128, 64], F32)
        k.nw_col = sb("nw_col", [128, 32], F32)
        k.gwt_bf = sb("gwt_bf", [128, 128], BF16)
        k.gates_bf = sb("gates_bf", [8, 2048], BF16)
        k.lg_all = k.buf2[:, 0:256].bitcast(F32)
        k.gt_small = sb("gt_small", [128, 16], F32)
        k.gates_tok = sb("gates_tok", [128, 8], F32)
        k.st1 = sb("st1", [128, 8], F32)
        k.spc = sb("spc", [128, 128], F32)
        k.pzT = k.P[0] if False else None
        k.wibdt = k.wib[1][:, 0:1024]
        k.rstd_bc = k.wib[0][:, 0:1024].bitcast(F32)
        k.xraw = [sb(f"xraw{i}", [128, 520], BF16) for i in range(2)]
        k.halo = sb("halo", [128, 102], BF16)
        k.wdg = [sb(f"wdg{i}", [128, 128], BF16) for i in range(4)]
        k.bf_bf = sb("bf_bf", [128, 512], BF16)
        k.cf_bf = sb("cf_bf", [128, 512], BF16)
        k.btok_bf = sb("btok_bf", [128, 512], BF16)
        k.dta0 = k.att_st[1][0:64, 0:512]
        k.dta1 = k.att_st[0][0:64, 0:512]
        k.dtf = k.wob[1][0:64, 0:1024].bitcast(F32)
        k.wtmp = k.wob[0][0:64, 0:1024].bitcast(F32)
        k.cum_lo = sb("cum_lo", [64, 512], BF16)
        k.chl = sb("chl", [128, 512], BF16)
        k.cum_hi = k.chl[0:64, :]
        k.nhl = sb("nhl", [128, 512], BF16)
        k.lnd_bf = sb("lnd_bf", [64, 512], BF16)
        k.utri = sb("utri", [128, 512], BF16)
        k.shid = sb("shid", [64, 128], BF16)
        k.small64 = sb("small64", [64, 1], F32)
        k.expcum_t = sb("expcum_t", [128, 256], F32)
        k.cum_t = k.expcum_t
        k.wt_bf = k.xraw[1][:, 0:256]
        k.atot4 = sb("atot4", [64, 4], F32)

        k.atot_bc = k.xraw[0][:, 0:512].bitcast(F32)
        k.ssq_t = sb("ssq_t", [128, 8], F32)
        k.rstd_t = sb("rstd_t", [128, 4], F32)
        k.rstd_row = None
        k.rstd_row = arz[0:1, 8192:9216].bitcast(F32)
        k.atot_row = k.rstd_row[0:1, 0:256]

        k.p = Sched(nc, ctx)
        block = ctx.enter_context(nc.Block())

        phase0(k)
        phase1(k)
        if stage >= 2:
            phase2(k)
        if stage >= 4:
            phase3(k)

        k.p.wait("sync", [("SC", k.p.cnt["SC"]), ("WG", k.p.cnt["WG"])])
        k.p.replay(block)
    return nc


def phase0(k):
    p, pb, P = k.p, k.pb, k.P
    k.ev_cw = p.dma("sync", k.cwcol[:].rearrange("p (cc kk) -> p cc kk", kk=4),
                    k.conv_w.rearrange("(cc p) kk -> p cc kk", p=128))
    k.ev_cb = p.dma("sync", k.cb_col[:], k.conv_b.rearrange("(cc p) -> p cc", p=128), slow=True)
    k.ev_dtb = p.dma("sync", k.dtb_col[:], k.dt_bias.rearrange("h -> h ()"))
    ev_alog = p.dma("sync", k.a_col[:], k.a_log.rearrange("h -> h ()"))
    ev_dv = p.dma("sync", k.d_row[:], k.dvec.rearrange("h -> () h"))
    k.ev_nw = p.dma("sync", k.nw_col[:], k.norm_w.rearrange("(kk p) -> p kk", p=128), slow=True)
    ev_gw = p.dma("gpsimd", k.gw_bf[:], k.gate_w)

    def g_ops(g):
        g.memset(k.ones1[:], 1.0)
        g.memset(k.one64[:], 1.0)
        g.memset(k.eps_col[:], EPS)
        g.memset(k.ident_bf[:], 0.0)
        g.affine_select(out=k.ident_bf[:], in_=k.ident_bf[:], compare_op=ALU.not_equal,
                        fill=1.0, base=0, pattern=[[-1, 128]], channel_multiplier=1)
        g.memset(k.ident_f[:], 0.0)
        g.affine_select(out=k.ident_f[:], in_=k.ident_f[:], compare_op=ALU.not_equal,
                        fill=1.0, base=0, pattern=[[-1, 128]], channel_multiplier=1)
        g.memset(k.tri01[:], 1.0)
        g.affine_select(out=k.tri01[:], in_=k.tri01[:], compare_op=ALU.is_ge,
                        fill=0.0, base=0, pattern=[[1, 128]], channel_multiplier=-1)
        g.memset(k.sel128[:], 0.0)
        g.affine_select(out=k.sel128[:].rearrange("kq (h m) -> kq h m", m=128),
                        in_=k.sel128[:].rearrange("kq (h m) -> kq h m", m=128),
                        compare_op=ALU.not_equal, fill=1.0, base=0,
                        pattern=[[-1, 64], [0, 128]], channel_multiplier=1)
        g.affine_select(out=k.sel128[:].rearrange("kq (h m) -> kq h m", m=128),
                        in_=k.sel128[:].rearrange("kq (h m) -> kq h m", m=128),
                        compare_op=ALU.not_equal, fill=1.0, base=-64,
                        pattern=[[-1, 64], [0, 128]], channel_multiplier=1)
        g.memset(k.utri[:], 0.0)
        g.affine_select(out=k.utri[:].rearrange("kq (h i) -> kq h i", i=128),
                        in_=k.utri[:].rearrange("kq (h i) -> kq h i", i=128),
                        compare_op=ALU.is_ge, fill=-30000.0, base=0,
                        pattern=[[0, 4], [1, 128]], channel_multiplier=-1)
        g.memset(k.small64[:], 1e-20)
        g.memset(k.shid[:], 0.0)
        g.affine_select(out=k.shid[:], in_=k.shid[:], compare_op=ALU.not_equal,
                        fill=1.0, base=-64, pattern=[[1, 128]], channel_multiplier=-1)
        g.memset(k.sel8[:], 0.0)
        g.affine_select(out=k.sel8[:].rearrange("kq (e m) -> kq e m", m=128),
                        in_=k.sel8[:].rearrange("kq (e m) -> kq e m", m=128),
                        compare_op=ALU.not_equal, fill=1.0, base=0,
                        pattern=[[-1, 8], [0, 128]], channel_multiplier=1)
        g.memset(k.halo[:], 0.0)
        g.memset(k.spc[:], 0.0)
        return g.memset(k.state[:], 0.0)
    k.ev_gp0 = p.op("gpsimd", g_ops, inc=True)

    p.op("scalar", lambda s: s.preload_act_table(AF.Exp))
    ev0 = p.op("scalar", lambda s: s.activation(out=k.a_col[:], in_=k.a_col[:], func=AF.Exp),
               deps=[ev_alog], inc=True)
    k.ev_acol = p.op("vector", lambda v: v.tensor_scalar(out=k.a_col[:], in0=k.a_col[:],
                                                         scalar1=-1.0, scalar2=None, op0=ALU.mult),
                     deps=[ev0], inc=True)
    evm = p.op("tensor", lambda t: t.matmul(P[0][:, 0:64], k.ones1[0:1, :], k.d_row[0:1, :],
                                            start=True, stop=True),
               deps=[ev_dv, k.ev_gp0] + pb.wdeps(BK(0, 0)), inc=True)
    pb.wrote(evm, BK(0, 0))
    k.ev_dbc = p.op("vector", lambda v: v.tensor_copy(k.dbc[:], P[0][:, 0:64]),
                    deps=pb.rdeps(BK(0, 0)) + [evm], inc=True)
    pb.read(k.ev_dbc, BK(0, 0))
    ev_gwt = None
    for kk in range(16):
        pv = P[1][:, 0:4].bitcast(BF16)
        evt = p.op("tensor",
                   lambda t, pv=pv, kk=kk: t.transpose(pv, k.gw_bf[0:8, kk * 128:(kk + 1) * 128],
                                                       k.ident_bf[0:8, 0:8]),
                   deps=([ev_gw, k.ev_gp0] if kk == 0 else []) + pb.wdeps(BK(1, 0)), inc=True)
        pb.wrote(evt, BK(1, 0))
        ev_gwt = p.op("vector",
                      lambda v, pv=pv, kk=kk: v.tensor_copy(k.gwt_bf[:, kk * 8:(kk + 1) * 8], pv),
                      deps=pb.rdeps(BK(1, 0)) + [evt], inc=True)
        pb.read(ev_gwt, BK(1, 0))
    k.ev_gwt = ev_gwt


def phase1(k):
    p, pb, P = k.p, k.pb, k.P
    p.op("scalar", lambda s: s.preload_act_table(AF.Sqrt))
    k.ev_xn_dma = []
    ev_lg = []
    ev_u_free = [None, None]
    ev_xnw_prev = None
    ev_lgmm_prev = None
    ev_xnT_prev = None
    ev_msq_prev = None
    for tc in range(16):
        sl = tc % 2
        ev_u = p.dma("sync", k.u_sb[sl][:], k.u[tc * 128:(tc + 1) * 128, :], deps=[ev_u_free[sl]])
        ev_mean = p.op("vector", lambda v, sl=sl: v.tensor_reduce(
            out=k.st1[:, 0:1], in_=k.u_sb[sl][:], axis=AX.X, op=ALU.add), deps=[ev_u], inc=True)
        ev_sq = p.op("scalar", lambda s, sl=sl: s.activation(out=k.usq[:], in_=k.u_sb[sl][:],
                                                             func=AF.Square),
                     deps=[ev_u, ev_msq_prev], inc=True)
        ev_msq = p.op("vector", lambda v: v.tensor_reduce(out=k.st1[:, 1:2], in_=k.usq[:],
                                                          axis=AX.X, op=ALU.add),
                      deps=[ev_sq], inc=True)
        ev_msq_prev = ev_msq
        def v_stats(v):
            v.tensor_scalar(out=k.st1[:, 0:1], in0=k.st1[:, 0:1], scalar1=1.0 / D, scalar2=None,
                            op0=ALU.mult)
            v.tensor_scalar(out=k.st1[:, 1:2], in0=k.st1[:, 1:2], scalar1=1.0 / D, scalar2=None,
                            op0=ALU.mult)
            v.tensor_tensor(out=k.st1[:, 2:3], in0=k.st1[:, 0:1], in1=k.st1[:, 0:1],
                            op=ALU.mult)
            SP(k, v)
            return v.scalar_tensor_tensor(out=k.st1[:, 2:3], in0=k.st1[:, 2:3], scalar=-1.0,
                                          in1=k.st1[:, 1:2], op0=ALU.mult, op1=ALU.add)
        ev_var = p.op("vector", v_stats, deps=[ev_msq], inc=True)
        ev_sqv = p.op("scalar", lambda s: s.activation(out=k.st1[:, 3:4], in_=k.st1[:, 2:3],
                                                       func=AF.Sqrt, bias=k.eps_col[:]),
                      deps=[ev_var], inc=True)
        ev_rst = p.op("vector", lambda v: v.reciprocal(k.st1[:, 3:4], k.st1[:, 3:4]),
                      deps=[ev_sqv], inc=True)
        def v_xn(v, sl=sl):
            SP(k, v)
            return v.tensor_scalar(
                out=k.xn_tok[:], in0=k.u_sb[sl][:], scalar1=k.st1[:, 0:1], scalar2=k.st1[:, 3:4],
                op0=ALU.subtract, op1=ALU.mult)
        ev_xn = p.op("vector", v_xn, deps=[ev_rst, ev_xnT_prev], inc=True)
        ev_u_free[sl] = ev_xn
        ev_ev4 = []
        for q in range(4):
            bk = BK(2, q % 2)
            pvq = P[2][:, (q % 2) * 512:(q % 2) * 512 + 256].bitcast(BF16)
            evt = None
            for j in range(4):
                kk = q * 4 + j
                evt = p.op("tensor",
                           lambda t, pvq=pvq, j=j, kk=kk: t.transpose(
                               pvq[:, j * 128:(j + 1) * 128],
                               k.xn_tok[:, kk * 128:(kk + 1) * 128], k.ident_bf[:]),
                           deps=([ev_xn] + pb.wdeps(bk)) if j == 0 else [], inc=(j == 3))
            pb.wrote(evt, bk)
            if q == 3:
                ev_xnT_prev = evt
            eve = p.op("vector",
                       lambda v, pvq=pvq, q=q: v.tensor_copy(
                           k.xn_stage[:].rearrange("p (kk t) -> p kk t", t=128)[:, q * 4:(q + 1) * 4, :],
                           pvq.rearrange("p (kk t) -> p kk t", t=128)),
                       deps=pb.rdeps(bk) + [evt, ev_xnw_prev if q == 0 else None,
                                            ev_lgmm_prev if q == 0 else None], inc=True)
            pb.read(eve, bk)
            ev_ev4.append(eve)
        ev_w = p.dma("sync", k.xn_scr.rearrange("(kk p) t -> p kk t", p=128)[:, :, tc * 128:(tc + 1) * 128],
                     k.xn_stage[:].rearrange("p (kk t) -> p kk t", t=128), deps=ev_ev4)
        k.ev_xn_dma.append(ev_w)
        ev_xnw_prev = ev_w
        evl = None
        for kk in range(16):
            evl = p.op("tensor",
                       lambda t, kk=kk: t.matmul(P[3][:, 0:8], k.xn_stage[:, kk * 128:(kk + 1) * 128],
                                                 k.gwt_bf[:, kk * 8:(kk + 1) * 8],
                                                 start=(kk == 0), stop=(kk == 15)),
                       deps=([ev_ev4[-1], k.ev_gwt] + pb.wdeps(BK(3, 0))) if kk == 0 else [],
                       inc=(kk == 15))
        pb.wrote(evl, BK(3, 0))
        ev_lgmm_prev = evl
        evle = p.op("vector", lambda v, tc=tc: v.tensor_copy(k.lg_all[:, tc * 8:(tc + 1) * 8],
                                                             P[3][:, 0:8]),
                    deps=pb.rdeps(BK(3, 0)) + [evl], inc=True)
        pb.read(evle, BK(3, 0))
        ev_lg.append(evle)

    p.op("scalar", lambda s: s.preload_act_table(AF.Exp))
    ev_gts = None
    ev_gT_prev = None
    for tc in range(16):
        lg = k.lg_all[:, tc * 8:(tc + 1) * 8]
        def v_g1(v, lg=lg):
            v.tensor_reduce(out=k.gt_small[:, 0:1], in_=lg, axis=AX.X, op=ALU.max)
            SP(k, v)
            v.tensor_scalar(out=k.gates_tok[:], in0=lg, scalar1=k.gt_small[:, 0:1],
                            scalar2=None, op0=ALU.is_equal)
            SP(k, v)
            v.scalar_tensor_tensor(out=k.gt_small[:, 8:16], in0=k.gates_tok[:], scalar=-1e9,
                                   in1=lg, op0=ALU.mult, op1=ALU.add)
            SP(k, v)
            v.tensor_reduce(out=k.gt_small[:, 1:2], in_=k.gt_small[:, 8:16], axis=AX.X, op=ALU.max)
            SP(k, v)
            return v.tensor_scalar(out=k.gt_small[:, 2:3], in0=k.gt_small[:, 1:2],
                                   scalar1=k.gt_small[:, 0:1], scalar2=None, op0=ALU.subtract)
        ev1 = p.op("vector", v_g1, deps=[ev_lg[tc], ev_gT_prev], inc=True)
        ev3 = p.op("scalar", lambda s: s.activation(out=k.gt_small[:, 3:4], in_=k.gt_small[:, 2:3],
                                                    func=AF.Exp), deps=[ev1], inc=True)
        def v_g2(v):
            v.tensor_scalar(out=k.gt_small[:, 3:4], in0=k.gt_small[:, 3:4], scalar1=1.0,
                            scalar2=None, op0=ALU.add)
            SP(k, v)
            v.reciprocal(k.gt_small[:, 4:5], k.gt_small[:, 3:4])
            v.tensor_scalar(out=k.gt_small[:, 8:16], in0=k.gt_small[:, 8:16],
                            scalar1=k.gt_small[:, 1:2], scalar2=None, op0=ALU.is_equal)
            v.tensor_scalar(out=k.gt_small[:, 5:6], in0=k.gt_small[:, 4:5], scalar1=-1.0,
                            scalar2=1.0, op0=ALU.mult, op1=ALU.add)
            v.tensor_scalar(out=k.gates_tok[:], in0=k.gates_tok[:], scalar1=k.gt_small[:, 4:5],
                            scalar2=None, op0=ALU.mult)
            SP(k, v)
            return v.scalar_tensor_tensor(out=k.gates_tok[:], in0=k.gt_small[:, 8:16],
                                          scalar=k.gt_small[:, 5:6], in1=k.gates_tok[:],
                                          op0=ALU.mult, op1=ALU.add)
        ev4 = p.op("vector", v_g2, deps=[ev3], inc=True)
        evt = p.op("tensor", lambda t: t.transpose(P[3][0:8, 512:640], k.gates_tok[:], k.ident_f[:]),
                   deps=[ev4] + pb.wdeps(BK(3, 1)), inc=True)
        pb.wrote(evt, BK(3, 1))
        ev_gT_prev = evt
        ev_gts = p.op("vector", lambda v, tc=tc: v.tensor_copy(
            k.gates_bf[:, tc * 128:(tc + 1) * 128], P[3][0:8, 512:640]),
            deps=pb.rdeps(BK(3, 1)) + [evt], inc=True)
        pb.read(ev_gts, BK(3, 1))
    k.ev_gts = ev_gts
    if k.debug:
        evc = p.op("vector", lambda v: v.tensor_copy(k.usq[0:8, 0:2048], k.gates_bf[:]),
                   deps=[ev_gts], inc=True)
        p.dma("sync", k.dbg_g, k.usq[0:8, 0:2048], deps=[evc])


def phase2(k):
    p, pb, P = k.p, k.pb, k.P
    S = k
    ev_wib_free = [None, None]
    ev_wibdt_free = None
    ev_xnt_free = None
    ev_zf_free = [k.ev_gwt, k.ev_gwt]
    ev_zst_free = [None] * 8
    ev_zrd_free = [None, None]
    ev_xraw_free = [None, None]
    ev_wdg_free = [None] * 4
    ev_xfbf_free = [k.ev_gwt, k.ev_gwt]
    ev_d8_free = [None, None]
    ev_e8_free = None
    ev_m8_free = [None, None]
    ev_stbf_free = [None, None]
    ev_wb8_free = [None, None]
    ev_id8_free = None
    ev_wob_free = [None, None]
    ev_attst_free = [None, None]
    ev_wout_done = None
    ev_tT = [None] * TPC
    ev_halo = [k.ev_gp0] * NCC
    ev_bf = ev_cf = ev_btok = None
    ev_state_upd = [k.ev_gp0] * 8
    ev_y_evac = None
    ev_ect_prev = None
    ev_ab_prev = None
    ev_Dmm_prev = None
    ev_stack_prev = None
    ev_rstdbc = None
    ev_sq2_prev = None
    ev_norm_prev = None
    ev_sqact_prev = None
    ev_dtraw = None
    ev_cbm_prev = None
    ev_rh_free = [k.ev_gts]
    evt_ = None
    eve = None
    eve8 = None
    ev_ssq = None
    evat = None
    ev_prep = None
    ev_dtft = None
    ev_wt = None
    ev_spl = None
    ev_w2 = None
    ev_ect = None
    cum = k.dta1
    k.ev_att_w = []
    zdma = [None] * 8
    zrd_ready = {}

    st = {}
    k.pzT = P[3][:, 640:896].bitcast(BF16)

    def zstage(m, tcl):
        r = m // 16
        return k.zst[tcl][:, (m - r * 16) * 128:(m - r * 16 + 1) * 128]

    def emit_win_m(ti, mg, chunkloop):
        nonlocal ev_wib_free, ev_wibdt_free, ev_dtraw, ev_ab_prev, ev_zf_free
        nonlocal ev_xraw_free, ev_wdg_free, ev_xfbf_free, ev_halo, ev_bf, ev_cf
        nonlocal ev_wout_done, ev_tT, ev_cbm_prev, ev_y_evac, eve, ev_zst_free
        nonlocal ev_xnt_free, ev_wob_free
        slw = mg % 2
        if mg < 33:
            ev_w = p.dma("gpsimd", k.wib[slw].rearrange("p (kk m) -> p kk m", m=256),
                         k.w_in.rearrange("(kk p) m -> p kk m", p=128)[:, :, mg * 256:(mg + 1) * 256],
                         deps=[ev_wib_free[slw], k.ev_gts if (ti == 0 and mg < 2) else None,
                               ev_wibdt_free if mg in (0, 1, 16, 17) else None])
            mos = (0, 1)
        else:
            ev_w = p.dma("gpsimd", k.wibdt[:].rearrange("p (kk m) -> p kk m", m=64),
                         k.w_in.rearrange("(kk p) m -> p kk m", p=128)[:, :, 8448:8512],
                         deps=[ev_wibdt_free, ev_wib_free[1]])
            mos = (0,)
        ev_last_mm = None
        for mo in mos:
            m = mg * 2 + mo
            isdt = mg == 33
            if chunkloop:
                slp = m % 2
                bk = BK(0, slp)
                pw = P[0][:, slp * 512:(slp + 1) * 512]
            else:
                slp = m % 4
                bk = BK(slp // 2, slp % 2)
                pw = P[slp // 2][:, (slp % 2) * 512:(slp % 2 + 1) * 512]
            pwv = pw[0:64, :] if isdt else pw
            evm = None
            for kk in range(16):
                lhsT = (k.wibdt[:, kk * 64:(kk + 1) * 64] if isdt
                        else k.wib[slw][:, kk * 256 + mo * 128: kk * 256 + mo * 128 + 128])
                evm = p.op("tensor",
                           lambda t, pwv=pwv, lhsT=lhsT, kk=kk:
                           t.matmul(pwv, lhsT, k.xn_tile[:, kk * 512:(kk + 1) * 512],
                                    start=(kk == 0), stop=(kk == 15)),
                           deps=([ev_w, st.get("ev_xnt")] + pb.wdeps(bk)) if kk == 0 else [],
                           inc=(kk == 15))
            pb.wrote(evm, bk)
            ev_last_mm = evm
            if isdt:
                ev_dtraw = p.op("vector", lambda v, pwv=pwv: v.tensor_copy(k.dta0[:], pwv),
                                deps=pb.rdeps(bk) + [evm, ev_ab_prev,
                                                     ev_attst_free[0], ev_attst_free[1]],
                                inc=True)
                pb.read(ev_dtraw, bk)
                st["ev_dtraw"] = ev_dtraw
            elif m < 32:
                slz = m % 2
                r = m // 16
                evc = p.op("vector", lambda v, pw=pw, slz=slz: v.tensor_copy(k.zf_bf[slz][:], pw),
                           deps=pb.rdeps(bk) + [evm, ev_zf_free[slz]], inc=True)
                pb.read(evc, bk)
                bt = BK(3, 1)
                pzT = k.pzT
                evt = None
                for tcl in range(4):
                    evt = p.op("tensor",
                               lambda t, pzT=pzT, tcl=tcl, slz=slz:
                               t.transpose(pzT[:, tcl * 128:(tcl + 1) * 128],
                                           k.zf_bf[slz][:, tcl * 128:(tcl + 1) * 128],
                                           k.ident_bf[:]),
                               deps=([evc] + pb.wdeps(bt)) if tcl == 0 else [], inc=(tcl == 3))
                pb.wrote(evt, bt)
                ev_zf_free[slz] = evt
                eve = None
                for tcl in range(4):
                    eve = p.op("vector",
                               lambda v, pzT=pzT, tcl=tcl, m=m:
                               v.tensor_copy(zstage(m, tcl), pzT[:, tcl * 128:(tcl + 1) * 128]),
                               deps=(pb.rdeps(bt) + [evt]
                                     + ([ev_zst_free[(1 - m // 16) * 4 + q] for q in range(4)]
                                        + ev_rh_free + [ev_wob_free[0], ev_wob_free[1]]
                                        if m % 16 == 0 else []))
                               if tcl == 0 else [],
                               inc=(tcl == 3))
                pb.read(eve, bt)
                if m % 16 == 15:
                    r = m // 16
                    for tcl in range(4):
                        zdma[r * 4 + tcl] = p.dma(
                            "sync",
                            k.z_scr[ti * 512 + tcl * 128: ti * 512 + (tcl + 1) * 128,
                                    r * 2048:(r + 1) * 2048],
                            k.zst[tcl][:], deps=[eve])
                        ev_zst_free[r * 4 + tcl] = zdma[r * 4 + tcl]
                        zrd_ready.setdefault((ti, tcl), []).append(zdma[r * 4 + tcl])
            else:
                cc = m - 32
                slr = cc % 2
                def v_xbc(v, pw=pw, cc=cc, slr=slr):
                    v.tensor_copy(k.xraw[slr][:, 0:3], k.halo[:, cc * 3:cc * 3 + 3])
                    v.tensor_copy(k.halo[:, cc * 3:cc * 3 + 3], pw[:, 509:512])
                    return v.tensor_copy(k.xraw[slr][:, 3:515], pw)
                evc = p.op("vector", v_xbc,
                           deps=pb.rdeps(bk) + [evm, ev_xraw_free[slr], ev_halo[cc]], inc=True)
                pb.read(evc, bk)
                ev_halo[cc] = evc
                evwd = []
                for kc in range(4):
                    e_ = p.op("vector",
                              lambda v, kc=kc, cc=cc: v.tensor_scalar(
                                  out=k.wdg[kc][:], in0=k.ident_bf[:],
                                  scalar1=k.cwcol[:, cc * 4 + kc:cc * 4 + kc + 1],
                                  scalar2=None, op0=ALU.mult),
                              deps=[ev_wdg_free[kc],
                                    k.ev_cw if (ti == 0 and cc == 0 and kc == 0) else None],
                              inc=True)
                    evwd.append(e_)
                slc = cc % 2
                bc_ = BK(3, slc)
                pcv = P[3][:, slc * 512:(slc + 1) * 512]
                evcm = None
                for kc in range(4):
                    evcm = p.op("tensor",
                                lambda t, pcv=pcv, kc=kc, slr=slr:
                                t.matmul(pcv, k.wdg[kc][:], k.xraw[slr][:, kc:kc + 512],
                                         start=(kc == 0), stop=(kc == 3)),
                                deps=([evc, evwd[3]] + pb.wdeps(bc_)) if kc == 0 else [],
                                inc=(kc == 3))
                pb.wrote(evcm, bc_)
                ev_xraw_free[slr] = evcm
                for kc in range(4):
                    ev_wdg_free[kc] = evcm
                if cc < 32:
                    dst, depx = k.xf_bf[cc % 2], ev_xfbf_free[cc % 2]
                elif cc == 32:
                    dst, depx = k.bf_bf, ev_cbm_prev
                else:
                    dst, depx = k.cf_bf, ev_y_evac
                evse = p.op("scalar",
                            lambda s, dst=dst, pcv=pcv, cc=cc:
                            s.activation(out=dst[:], in_=pcv, func=AF.Silu,
                                         bias=k.cb_col[:, cc:cc + 1]),
                            deps=pb.rdeps(bc_) + [evcm, depx,
                                                  k.ev_cb if (ti == 0 and cc == 0) else None],
                            inc=True)
                pb.read(evse, bc_)
                if cc == 32:
                    ev_bf = evse
                    st["ev_bf"] = evse
                elif cc == 33:
                    ev_cf = evse
                    st["ev_cf"] = evse
                else:
                    slt = cc % 2
                    bt = BK(2, slt)
                    pxT = P[2][:, slt * 512: slt * 512 + 256].bitcast(BF16)
                    evt = None
                    for tcl in range(4):
                        evt = p.op("tensor",
                                   lambda t, pxT=pxT, tcl=tcl, cc=cc:
                                   t.transpose(pxT[:, tcl * 128:(tcl + 1) * 128],
                                               k.xf_bf[cc % 2][:, tcl * 128:(tcl + 1) * 128],
                                               k.ident_bf[:]),
                                   deps=([evse] + pb.wdeps(bt)) if tcl == 0 else [],
                                   inc=(tcl == 3))
                    pb.wrote(evt, bt)
                    ev_xfbf_free[cc % 2] = evt
                    eve = None
                    for tcl in range(4):
                        eve = p.op("vector",
                                   lambda v, pxT=pxT, tcl=tcl, cc=cc:
                                   v.tensor_copy(k.buf2[:, tcl * 4096 + cc * 128: tcl * 4096 + (cc + 1) * 128],
                                                 pxT[:, tcl * 128:(tcl + 1) * 128]),
                                   deps=(pb.rdeps(bt) + [evt, ev_wout_done,
                                                         ev_tT[0], ev_tT[1], ev_tT[2], ev_tT[3]])
                                   if tcl == 0 else [],
                                   inc=(tcl == 3))
                    pb.read(eve, bt)
                    if cc == 0:
                        ev_wout_done = None
                        ev_tT = [None] * TPC
        if mg < 33:
            ev_wib_free[slw] = ev_last_mm
        else:
            ev_wibdt_free = ev_last_mm
            ev_xnt_free = ev_last_mm
            st["ev_xnt_free"] = ev_last_mm

    def emit_prep(ti):
        nonlocal ev_ab_prev, ev_Dmm_prev, ev_ect_prev, ev_prep, ev_dtft, ev_wt
        nonlocal ev_spl, ev_w2, ev_ect, ev_y_evac, ev_wob_free, ev_stack_prev
        ev_dtraw_ = st["ev_dtraw"]
        p.op("scalar", lambda s: s.preload_act_table(AF.Exp))
        ev_t1 = p.op("vector", lambda v: v.tensor_scalar(out=k.dta0[:], in0=k.dta0[:],
                                                         scalar1=k.dtb_col[:], scalar2=None,
                                                         op0=ALU.add),
                     deps=[ev_dtraw_, k.ev_dtb if ti == 0 else None], inc=True)
        ev_ab = p.op("scalar", lambda s: s.activation(out=k.dtf[:], in_=k.dta0[:], func=AF.Abs),
                     deps=[ev_t1, ev_wob_free[1]]
                     + [z for z in zdma[2:4] + zdma[6:8] if z is not None], inc=True)
        ev_ab_prev = ev_ab
        ev_ex = p.op("scalar", lambda s: s.activation(out=k.dtf[:], in_=k.dtf[:], func=AF.Exp,
                                                      scale=-1.0), deps=[ev_ab], inc=True)
        p.op("scalar", lambda s: s.preload_act_table(AF.Ln))
        ev_ln = p.op("scalar", lambda s: s.activation(out=k.dtf[:], in_=k.dtf[:], func=AF.Ln,
                                                      bias=k.one64[:]), deps=[ev_ex], inc=True)
        def v_dtf(v):
            v.tensor_scalar(out=k.dta0[:], in0=k.dta0[:], scalar1=0.0, scalar2=None, op0=ALU.max)
            return v.tensor_tensor(out=k.dtf[:], in0=k.dtf[:], in1=k.dta0[:], op=ALU.add)
        ev_dtfv = p.op("vector", v_dtf, deps=[ev_ln], inc=True)
        ev_lnd = p.op("scalar", lambda s: s.activation(out=k.lnd_bf[:], in_=k.dtf[:],
                                                       func=AF.Ln, bias=k.small64[:]),
                      deps=[ev_dtfv, ev_Dmm_prev, k.ev_gp0 if ti == 0 else None], inc=True)
        st["ev_lnd"] = ev_lnd
        p.op("scalar", lambda s: s.preload_act_table(AF.Exp))
        ev_da = p.op("vector", lambda v: v.tensor_scalar(out=k.dta0[:], in0=k.dtf[:],
                                                         scalar1=k.a_col[:], scalar2=None,
                                                         op0=ALU.mult),
                     deps=[ev_dtfv, k.ev_acol if ti == 0 else None, ev_Dmm_prev], inc=True)
        ev_dbgd = None
        if k.debug and k.stage == 3 and ti == 0:
            p.dma("sync", k.dbg_p[0:64, 2624:3136], k.dtf[:], deps=[ev_dtfv])
            ev_dbgd = p.dma("sync", k.dbg_p[64:128, 2624:3136], k.dta0[:], deps=[ev_da])
        bufs = [k.dta0, k.dta1]
        ev_cs = None
        for si, s_ in enumerate([1, 2, 4, 8, 16, 32, 64]):
            def v_cs(v, s_=s_, si=si):
                src, dst = bufs[si % 2], bufs[1 - si % 2]
                v.tensor_copy(dst[:], src[:])
                return v.tensor_tensor(
                    out=dst[:].rearrange("p (c kq) -> p c kq", kq=128)[:, :, s_:],
                    in0=src[:].rearrange("p (c kq) -> p c kq", kq=128)[:, :, s_:],
                    in1=src[:].rearrange("p (c kq) -> p c kq", kq=128)[:, :, :128 - s_],
                    op=ALU.add)
            ev_cs = p.op("vector", v_cs,
                         deps=[ev_attst_free[0], ev_attst_free[1], ev_dbgd] if si == 0 else [],
                         inc=(s_ == 64))
        def v_w1(v):
            SP(k, v)
            e_ = None
            for c in range(4):
                e_ = v.tensor_scalar(out=k.wtmp[:, c * 128:(c + 1) * 128],
                                     in0=cum[:, c * 128:(c + 1) * 128],
                                     scalar1=cum[:, c * 128 + 127:c * 128 + 128],
                                     scalar2=-1.0, op0=ALU.subtract, op1=ALU.mult)
            return e_
        ev_w1 = p.op("vector", v_w1, deps=[ev_cs, ev_wob_free[0]]
                     + [z for z in zdma[0:2] + zdma[4:6] if z is not None], inc=True)
        ev_w2 = p.op("scalar", lambda s: s.activation(out=k.wtmp[:], in_=k.wtmp[:], func=AF.Exp),
                     deps=[ev_w1], inc=True)
        ev_w3 = p.op("vector", lambda v: v.tensor_tensor(out=k.dta0[:], in0=k.wtmp[:], in1=k.dtf[:],
                                                         op=ALU.mult), deps=[ev_w2], inc=True)
        wdl = k.dta0
        ev_a4 = p.op("scalar", lambda s: s.activation(out=k.atot4[:], in_=cum[:, 127:512:128],
                                                      func=AF.Exp), deps=[ev_cs], inc=True)
        def v_split(v):
            v.tensor_copy(k.cum_hi[:], cum[:])
            v.tensor_tensor(out=k.wtmp[:], in0=cum[:], in1=k.cum_hi[:], op=ALU.subtract)
            return v.tensor_copy(k.cum_lo[:], k.wtmp[:])
        ev_spl = p.op("vector", v_split, deps=[ev_w3, ev_stack_prev, ev_Dmm_prev], inc=True)
        st["ev_spl"] = ev_spl
        st["ev_w2"] = ev_w2
        bq = BK(3, 1)
        evs = p.op("tensor", lambda t: t.matmul(P[3][:, 512:1024], k.ident_bf[0:64, :],
                                                k.cum_hi[:], start=True, stop=False),
                   deps=[ev_spl] + pb.wdeps(bq), inc=False)
        evs = p.op("tensor", lambda t: t.matmul(P[3][:, 512:1024], k.shid[:],
                                                k.cum_lo[:], start=False, stop=True),
                   deps=[], inc=True)
        pb.wrote(evs, bq)
        ev_stack_prev = evs
        def v_chl(v):
            v.tensor_copy(k.chl[:], P[3][:, 512:1024])
            return v.tensor_scalar(out=k.nhl[:], in0=P[3][:, 512:1024], scalar1=-1.0,
                                   scalar2=None, op0=ALU.mult)
        ev_chl = p.op("vector", v_chl, deps=pb.rdeps(bq) + [evs, ev_Dmm_prev], inc=True)
        pb.read(ev_chl, bq)
        st["ev_chl"] = ev_chl
        evt = None
        for c in range(4):
            evt = p.op("tensor",
                       lambda t, c=c: t.transpose(P[3][:, 512 + c * 64:512 + (c + 1) * 64],
                                                  cum[:, c * 128:(c + 1) * 128],
                                                  k.ident_f[0:64, 0:64]),
                       deps=([ev_cs] + pb.wdeps(bq)) if c == 0 else [], inc=(c == 3))
        pb.wrote(evt, bq)
        ev_pr = p.op("vector", lambda v: v.tensor_copy(k.cum_t[:], P[3][:, 512:768]),
                     deps=pb.rdeps(bq) + [evt, ev_ect_prev], inc=True)
        pb.read(ev_pr, bq)
        ev_ect = p.op("scalar", lambda s: s.activation(out=k.expcum_t[:], in_=k.cum_t[:],
                                                       func=AF.Exp),
                      deps=[ev_pr, ev_y_evac], inc=True)
        ev_ect_prev = ev_ect
        st["ev_ect"] = ev_ect
        evt = None
        for c in range(4):
            evt = p.op("tensor",
                       lambda t, c=c: t.transpose(P[3][:, 512 + c * 64:512 + (c + 1) * 64],
                                                  wdl[:, c * 128:(c + 1) * 128],
                                                  k.ident_f[0:64, 0:64]),
                       deps=([ev_w3] + pb.wdeps(bq)) if c == 0 else [], inc=(c == 3))
        pb.wrote(evt, bq)
        ev_wt = p.op("vector", lambda v: v.tensor_copy(k.wt_bf[:], P[3][:, 512:768]),
                     deps=pb.rdeps(bq) + [evt], inc=True)
        pb.read(ev_wt, bq)
        st["ev_wt"] = ev_wt
        evt = None
        for c in range(4):
            evt = p.op("tensor",
                       lambda t, c=c: t.transpose(P[3][0:1, 768 + c * 64:768 + (c + 1) * 64],
                                                  k.atot4[:, c:c + 1], k.ident_f[0:64, 0:64]),
                       deps=([ev_a4] + pb.wdeps(bq)) if c == 0 else [], inc=(c == 3))
        pb.wrote(evt, bq)
        ev_at = p.op("vector", lambda v: v.tensor_copy(k.atot_row[:], P[3][0:1, 768:1024]),
                     deps=pb.rdeps(bq) + [evt], inc=True)
        pb.read(ev_at, bq)
        evm2 = None
        for c in range(4):
            evm2 = p.op("tensor", lambda t, c=c: t.matmul(P[3][:, 512 + c * 64:512 + (c + 1) * 64],
                                                          k.ones1[0:1, :],
                                                          k.atot_row[0:1, c * 64:(c + 1) * 64],
                                                          start=True, stop=True),
                        deps=([ev_at] + pb.wdeps(bq)) if c == 0 else [], inc=(c == 3))
        pb.wrote(evm2, bq)
        ev_prep = p.op("vector", lambda v: v.tensor_copy(k.atot_bc[:], P[3][:, 512:768]),
                       deps=pb.rdeps(bq) + [evm2], inc=True)
        pb.read(ev_prep, bq)
        st["ev_prep"] = ev_prep
        if k.debug and k.stage == 3 and ti == 0:
            p.dma("sync", k.dbg_p[0:64, 0:256], k.cum_hi[:].bitcast(F32), deps=[ev_spl])
            p.dma("sync", k.dbg_p[0:64, 256:512], k.cum_lo[:].bitcast(F32), deps=[ev_spl])
            p.dma("sync", k.dbg_p[0:64, 512:640], k.lnd_bf[:, 0:256].bitcast(F32), deps=[ev_lnd])
            p.dma("sync", k.dbg_p[0:128, 640:768], k.wt_bf[:].bitcast(F32), deps=[ev_wt])
            p.dma("sync", k.dbg_p[0:128, 768:1024], k.atot_bc[:], deps=[ev_prep])
            p.dma("sync", k.dbg_p[0:64, 1024:1536], k.dta1[:], deps=[ev_spl])

    def emit_chunk_start(ci):
        nonlocal ev_cbm_prev
        tcl = ci % TPC
        slzr = ci % 2
        ev_zr = p.dma("sync", k.z_rd[slzr][:], k.z_scr[ci * 128:(ci + 1) * 128, :],
                      deps=[ev_zrd_free[slzr]] + zrd_ready.get((ci // TPC, tcl), []))
        st["ev_zr"] = ev_zr
        bcb = BK(3, 1)
        evcb = p.op("tensor", lambda t, tcl=tcl: t.matmul(
            P[3][:, 512:640], k.bf_bf[:, tcl * 128:(tcl + 1) * 128],
            k.cf_bf[:, tcl * 128:(tcl + 1) * 128], start=True, stop=True),
            deps=[st["ev_bf"], st["ev_cf"], st["ev_prep"] if tcl == 0 else None]
            + pb.wdeps(bcb), inc=True)
        pb.wrote(evcb, bcb)
        evcbe = p.op("vector", lambda v: v.tensor_tensor(out=k.cbm_bf[:], in0=P[3][:, 512:640],
                                                         in1=k.tri01[:], op=ALU.mult),
                     deps=pb.rdeps(bcb) + [evcb, ev_cbm_prev,
                                           k.ev_gts if ci == 0 else None], inc=True)
        pb.read(evcbe, bcb)
        ev_cbm_prev = evcbe
        st["evcbe"] = evcbe

    def emit_group(ci, gg):
        nonlocal ev_Dmm_prev, ev_d8_free, ev_e8_free, ev_m8_free, ev_id8_free
        nonlocal ev_stbf_free, ev_wb8_free, ev_y_evac, ev_state_upd, eve8, ev_sqact_prev
        tcl = ci % TPC
        slg = gg % 2
        pD = P[1]
        bD = (BK(1, 0), BK(1, 1))
        evd = None
        for half in range(2):
            evd = p.op("tensor",
                       lambda t, pD=pD, half=half, gg=gg, tcl=tcl:
                       t.matmul(pD[:, half * 512:(half + 1) * 512],
                                k.nhl[:, tcl * 128:(tcl + 1) * 128],
                                k.sel128[:, gg * 1024 + half * 512: gg * 1024 + (half + 1) * 512],
                                start=True, stop=False),
                       deps=(pb.wdeps(*bD) + [st["ev_chl"],
                                              st["ev_lnd"] if gg == 0 else None,
                                              st["evcbe"] if gg == 0 else None])
                       if half == 0 else [],
                       inc=False)
        for half in range(2):
            evd = p.op("tensor",
                       lambda t, pD=pD, half=half, gg=gg, tcl=tcl:
                       t.matmul(pD[:, half * 512:(half + 1) * 512],
                                k.lnd_bf[:, tcl * 128:(tcl + 1) * 128],
                                k.sel64[:, gg * 1024 + half * 512: gg * 1024 + (half + 1) * 512],
                                start=False, stop=False),
                       deps=[], inc=False)
        for half in range(2):
            evd = p.op("tensor",
                       lambda t, pD=pD, half=half:
                       t.matmul(pD[:, half * 512:(half + 1) * 512], k.ident_bf[:],
                                k.utri[:], start=False, stop=False),
                       deps=[], inc=False)
        for hp in range(8):
            evd = p.op("tensor",
                       lambda t, pD=pD, hp=hp, gg=gg, tcl=tcl:
                       t.matmul(pD[:, hp * 128:(hp + 1) * 128],
                                k.sel128[:, (gg * 8 + hp) * 128:(gg * 8 + hp + 1) * 128],
                                k.chl[:, tcl * 128:(tcl + 1) * 128],
                                start=False, stop=(hp in (3, 7))),
                       deps=[], inc=(hp == 7))
        pb.wrote(evd, *bD)
        ev_Dmm_prev = evd
        eve8 = p.op("scalar", lambda s, pD=pD: s.activation(
            out=k.e8[:], in_=pD[:], func=AF.Exp),
            deps=pb.rdeps(*bD) + [evd, ev_e8_free,
                                  st["ev_prep"] if (tcl == 0 and gg < 2) else None],
            inc=True)
        pb.read(eve8, *bD)
        ev_sqact_prev = eve8
        evid = p.op("vector", lambda v, gg=gg: v.tensor_tensor(
            out=k.id8[:].rearrange("p (h i) -> p h i", i=128),
            in0=k.ident_bf[:].rearrange("p i -> p () i").broadcast_to([128, 8, 128]),
            in1=k.dbc[:, gg * 8:(gg + 1) * 8].rearrange("p h -> p h ()")
            .broadcast_to([128, 8, 128]),
            op=ALU.mult),
            deps=[ev_id8_free, k.ev_dbc if (ci == 0 and gg == 0) else None], inc=True)
        def v_m8(v, slg=slg):
            v.tensor_tensor(out=k.m8[slg][:].rearrange("p (h i) -> p h i", i=128),
                            in0=k.e8[:].rearrange("p (h i) -> p h i", i=128),
                            in1=k.cbm_bf[:].rearrange("p i -> p () i")
                            .broadcast_to([128, 8, 128]),
                            op=ALU.mult)
            return v.tensor_tensor(out=k.m8[slg][:], in0=k.m8[slg][:], in1=k.id8[:],
                                   op=ALU.add)
        evm8 = p.op("vector", v_m8,
                    deps=[eve8, evid, ev_m8_free[slg]],
                    inc=True)
        if k.debug and k.stage == 3 and ci == 0 and gg == 0:
            p.dma("sync", k.dbg_p[:, 1536:2048], k.m8[slg][:].bitcast(F32), deps=[evm8])
            p.dma("sync", k.dbg_p[:, 2048:2112], k.cbm_bf[:].bitcast(F32), deps=[evm8])
            p.dma("sync", k.dbg_p[:, 2112:2624], k.e8[:].bitcast(F32), deps=[evm8])
            p.dma("sync", k.dbg_p[:, 3136:4096],
                  k.buf2[:, 0:1920].bitcast(F32), deps=[evm8])

        ev_e8_free = evm8
        ev_id8_free = evm8
        evsb = None
        if ci > 0:
            evsb = p.op("vector", lambda v, gg=gg, slg=slg: v.tensor_copy(
                k.st_bf[slg][:], k.state[:, gg * 512:(gg + 1) * 512]),
                deps=[ev_stbf_free[slg]], inc=True)
        evwb = p.op("vector", lambda v, slg=slg, gg=gg, tcl=tcl: v.tensor_tensor(
            out=k.wb8[slg][:].rearrange("p (h n) -> p h n", n=128),
            in0=k.btok_bf[:, tcl * 128:(tcl + 1) * 128]
            .rearrange("p n -> p () n").broadcast_to([128, 8, 128]),
            in1=k.wt_bf[:, tcl * 64 + gg * 8: tcl * 64 + (gg + 1) * 8]
            .rearrange("p h -> p h ()").broadcast_to([128, 8, 128]),
            op=ALU.mult),
            deps=[ev_wb8_free[slg], st.get("ev_btok") if gg == 0 else None,
                  st["ev_wt"] if gg == 0 else None], inc=True)
        bYd, bYo, bst_ = BK(2, 0), BK(2, 1), BK(3, 0)
        evyd = None
        for hp in range(8):
            h = gg * 8 + hp
            evyd = p.op("tensor",
                        lambda t, hp=hp, h=h, slg=slg, tcl=tcl:
                        t.matmul(P[2][:, hp * 64:(hp + 1) * 64],
                                 k.m8[slg][:, hp * 128:(hp + 1) * 128],
                                 k.buf2[:, tcl * 4096 + h * 64: tcl * 4096 + (h + 1) * 64],
                                 start=True, stop=True),
                        deps=([evm8] + pb.wdeps(bYd)) if hp == 0 else [], inc=(hp == 7))
        pb.wrote(evyd, bYd)
        ev_m8_free[slg] = evyd
        evyo = None
        if ci > 0:
            for hp in range(8):
                evyo = p.op("tensor",
                            lambda t, hp=hp, slg=slg, tcl=tcl:
                            t.matmul(P[2][:, 512 + hp * 64: 512 + (hp + 1) * 64],
                                     k.cf_bf[:, tcl * 128:(tcl + 1) * 128],
                                     k.st_bf[slg][:, hp * 64:(hp + 1) * 64],
                                     start=True, stop=True),
                            deps=([evsb] + pb.wdeps(bYo)) if hp == 0 else [],
                            inc=(hp == 7))
            pb.wrote(evyo, bYo)
            ev_stbf_free[slg] = evyo
        evst = None
        for hp in range(8):
            h = gg * 8 + hp
            evst = p.op("tensor",
                        lambda t, hp=hp, h=h, slg=slg, tcl=tcl:
                        t.matmul(P[3][:, hp * 64:(hp + 1) * 64],
                                 k.wb8[slg][:, hp * 128:(hp + 1) * 128],
                                 k.buf2[:, tcl * 4096 + h * 64: tcl * 4096 + (h + 1) * 64],
                                 start=True, stop=True),
                        deps=([evwb] + pb.wdeps(bst_)) if hp == 0 else [], inc=(hp == 7))
        pb.wrote(evst, bst_)
        ev_wb8_free[slg] = evst
        st["evst"] = evst
        if ci > 0:
            def v_ye(v, gg=gg, tcl=tcl):
                v.tensor_tensor(out=k.y_bf[:, gg * 512:(gg + 1) * 512]
                                .rearrange("p (h q) -> p h q", q=64),
                                in0=P[2][:, 512:1024].rearrange("p (h q) -> p h q", q=64),
                                in1=k.expcum_t[:, tcl * 64 + gg * 8: tcl * 64 + (gg + 1) * 8]
                                .rearrange("p h -> p h ()").broadcast_to([128, 8, 64]),
                                op=ALU.mult)
                return v.tensor_tensor(out=k.y_bf[:, gg * 512:(gg + 1) * 512],
                                       in0=k.y_bf[:, gg * 512:(gg + 1) * 512],
                                       in1=P[2][:, 0:512], op=ALU.add)
            ev_y_evac = p.op("vector", v_ye,
                             deps=pb.rdeps(bYd, bYo) + [evyd, evyo,
                                                        st["ev_ect"] if gg == 0 else None,
                                                        ev_norm_prev if gg == 0 else None],
                             inc=True)
            pb.read(ev_y_evac, bYd, bYo)
        else:
            ev_y_evac = p.op("vector", lambda v, gg=gg: v.tensor_copy(
                k.y_bf[:, gg * 512:(gg + 1) * 512], P[2][:, 0:512]),
                deps=pb.rdeps(bYd) + [evyd, ev_norm_prev if gg == 0 else None], inc=True)
            pb.read(ev_y_evac, bYd)
        def v_su(v, gg=gg, tcl=tcl):
            v.tensor_tensor(out=k.state[:, gg * 512:(gg + 1) * 512]
                            .rearrange("p (h q) -> p h q", q=64),
                            in0=k.state[:, gg * 512:(gg + 1) * 512]
                            .rearrange("p (h q) -> p h q", q=64),
                            in1=k.atot_bc[:, tcl * 64 + gg * 8: tcl * 64 + (gg + 1) * 8]
                            .rearrange("p h -> p h ()").broadcast_to([128, 8, 64]),
                            op=ALU.mult)
            return v.tensor_tensor(out=k.state[:, gg * 512:(gg + 1) * 512],
                                   in0=k.state[:, gg * 512:(gg + 1) * 512],
                                   in1=P[3][:, 0:512], op=ALU.add)
        if k.debug and k.stage == 3 and ci == 0 and gg == 0:
            p.dma("sync", k.dbg_p[0:128, 1024:1280], k.y_bf[:, 0:512].bitcast(F32),
                  deps=[ev_y_evac])
        ev_state_upd[gg] = p.op("vector", v_su,
                                deps=pb.rdeps(bst_) + [evst,
                                                       st["ev_prep"] if gg == 0 else None],
                                inc=True)
        pb.read(ev_state_upd[gg], bst_)
        if ci == 0:
            ev_stbf_free[slg] = ev_state_upd[gg]

    def emit_chunk_end(ci):
        nonlocal ev_zrd_free, ev_norm_prev, ev_sqact_prev, ev_ssq, ev_tT, evt_
        tcl = ci % TPC
        slzr = ci % 2
        evth = p.op("scalar", lambda s, slzr=slzr: s.activation(
            out=k.th_bf[:], in_=k.z_rd[slzr][:], func=AF.Tanh, scale=0.5),
            deps=[st["ev_zr"], ev_norm_prev], inc=True)
        def v_norm(v, slzr=slzr):
            v.tensor_scalar(out=k.th_bf[:], in0=k.th_bf[:], scalar1=0.5, scalar2=0.5,
                            op0=ALU.mult, op1=ALU.add)
            v.tensor_tensor(out=k.th_bf[:], in0=k.th_bf[:], in1=k.z_rd[slzr][:], op=ALU.mult)
            return v.tensor_tensor(out=k.t_bf[:], in0=k.th_bf[:], in1=k.y_bf[:], op=ALU.mult)
        evt_ = p.op("vector", v_norm, deps=[evth, ev_y_evac, ev_sqact_prev], inc=True)
        ev_zrd_free[slzr] = evt_
        evsq = p.op("scalar", lambda s: s.activation(out=k.th_bf[:], in_=k.t_bf[:],
                                                     func=AF.Square), deps=[evt_], inc=True)
        ev_sqact_prev = evsq
        ev_ssq = p.op("vector", lambda v, tcl=tcl: v.tensor_reduce(
            out=k.ssq_t[:, tcl:tcl + 1], in_=k.th_bf[:], axis=AX.X, op=ALU.add),
            deps=[evsq, ev_sq2_prev if tcl == 0 else None], inc=True)
        ev_norm_prev = ev_ssq
        st["ev_ssq"] = ev_ssq
        evtT = None
        for qq in range(8):
            slq = qq % 2
            btq = BK(2, slq)
            ptT = [P[2][:, 0:256].bitcast(BF16), P[2][:, 512:768].bitcast(BF16)][slq]
            evtr = None
            for j in range(4):
                kk = qq * 4 + j
                evtr = p.op("tensor",
                            lambda t, ptT=ptT, j=j, kk=kk: t.transpose(
                                ptT[:, j * 128:(j + 1) * 128],
                                k.t_bf[:, kk * 128:(kk + 1) * 128], k.ident_bf[:]),
                            deps=([evt_, st["evst"]] + pb.wdeps(btq)) if j == 0 else [],
                            inc=(j == 3))
            pb.wrote(evtr, btq)
            for j in range(4):
                kk = qq * 4 + j
                evtT = p.op("vector",
                            lambda v, ptT=ptT, j=j, kk=kk, tcl=tcl: v.tensor_scalar(
                                out=k.buf2[:, tcl * 4096 + kk * 128: tcl * 4096 + (kk + 1) * 128],
                                in0=ptT[:, j * 128:(j + 1) * 128],
                                scalar1=k.nw_col[:, kk:kk + 1], scalar2=None, op0=ALU.mult),
                            deps=[evtr, k.ev_nw if (ci == 0 and kk == 0) else None]
                            if j == 0 else [],
                            inc=(j == 3))
            pb.read(evtT, btq)
        ev_tT[tcl] = evtT

    def emit_rstd_wout(ti):
        nonlocal ev_sq2_prev, ev_wob_free, ev_attst_free, ev_wout_done, ev_wibdt_free, evat
        nonlocal ev_rstdbc
        p.op("scalar", lambda s: s.preload_act_table(AF.Sqrt))
        ev_sq2 = p.op("scalar", lambda s: s.activation(out=k.rstd_t[:], in_=k.ssq_t[:, 0:4],
                                                       func=AF.Sqrt, bias=k.eps_col[:],
                                                       scale=1.0 / DIN),
                      deps=[st["ev_ssq"]], inc=True)
        ev_sq2_prev = ev_sq2
        ev_rc = p.op("vector", lambda v: v.reciprocal(k.rstd_t[:], k.rstd_t[:]),
                     deps=[ev_sq2], inc=True)
        bq = BK(3, 1)
        evt = None
        for c in range(4):
            evt = p.op("tensor", lambda t, c=c: t.transpose(
                P[3][0:1, 512 + c * 128: 512 + c * 128 + 128], k.rstd_t[:, c:c + 1], k.ident_f[:]),
                deps=([ev_rc] + pb.wdeps(bq)) if c == 0 else [], inc=(c == 3))
        pb.wrote(evt, bq)
        ev_rr = p.op("vector", lambda v: v.tensor_copy(k.rstd_row[0:1, :], P[3][0:1, 512:1024]),
                     deps=pb.rdeps(bq) + [evt], inc=True)
        pb.read(ev_rr, bq)
        bbc = BK(2, 0)
        evbc = p.op("tensor", lambda t: t.matmul(P[2][:, 0:512], k.ones1[0:1, :],
                                                 k.rstd_row[0:1, :], start=True, stop=True),
                    deps=[ev_rr] + pb.wdeps(bbc), inc=True)
        pb.wrote(evbc, bbc)
        ev_rstdbc = p.op("vector", lambda v: v.tensor_copy(k.rstd_bc[:, 0:512], P[2][:, 0:512]),
                         deps=pb.rdeps(bbc) + [evbc, ev_wib_free[0], ev_wibdt_free], inc=True)
        pb.read(ev_rstdbc, bbc)
        for m in range(16):
            slw = m % 2
            p.dma("gpsimd", k.wob[slw].rearrange("p (kk dd) -> p kk dd", dd=128)[:, 0:16, :],
                  k.w_out.rearrange("(kk p) dd -> p kk dd", p=128)[:, 0:16, m * 128:(m + 1) * 128],
                  deps=[ev_wob_free[slw], k.ev_gts if (ti == 0 and m < 2) else None,
                        (st.get("ev_spl") if m < 2 else None),
                        (st.get("ev_lnd") if m < 2 else None),
                        (st.get("ev_w2") if m < 2 else None),
                        (st.get("ev_wt") if m < 2 else None)]
                  + ([z for z in zdma if z is not None] if m < 2 else []))
            evw2 = p.dma("gpsimd", k.wob[slw].rearrange("p (kk dd) -> p kk dd", dd=128)[:, 16:32, :],
                         k.w_out.rearrange("(kk p) dd -> p kk dd", p=128)[:, 16:32, m * 128:(m + 1) * 128],
                         deps=[])
            slp = m % 2
            bam = BK(0, slp)
            pam = P[0][:, slp * 512:(slp + 1) * 512].rearrange("p (c t) -> p c t", t=128)
            evm = None
            for kk in range(32):
                rhs = k.buf2[:].rearrange("p (c f) -> p c f", f=4096)[:, :, kk * 128:(kk + 1) * 128]
                evm = p.op("tensor",
                           lambda t, pam=pam, kk=kk, slw=slw, rhs=rhs:
                           t.matmul(pam, k.wob[slw][:, kk * 128:(kk + 1) * 128], rhs,
                                    start=(kk == 0), stop=(kk == 31)),
                           deps=([evw2, ev_tT[0], ev_tT[1], ev_tT[2], ev_tT[3]]
                                 + pb.wdeps(bam)) if kk == 0 else [],
                           inc=(kk == 31))
            pb.wrote(evm, bam)
            ev_wob_free[slw] = evm
            if m == 15:
                ev_wout_done = evm
            evat = p.op("vector", lambda v, slp=slp, slw=slw: v.tensor_tensor(
                out=k.att_st[slw][:], in0=P[0][:, slp * 512:(slp + 1) * 512],
                in1=k.rstd_bc[:, 0:512], op=ALU.mult),
                deps=pb.rdeps(bam) + [evm, ev_rstdbc, ev_attst_free[slw]], inc=True)
            pb.read(evat, bam)
            evad = p.dma("sync", k.att_scr[m * 128:(m + 1) * 128, ti * 512:(ti + 1) * 512],
                         k.att_st[slw][:], deps=[evat])
            ev_attst_free[slw] = evad
            k.ev_att_w.append(evad)
        ev_wibdt_free = evat

    for step in range(NT + 1):
        if step < NT:
            ev_xnt = p.dma("sync", k.xn_tile.rearrange("p (kk t) -> p kk t", t=512),
                           k.xn_scr.rearrange("(kk p) t -> p kk t", p=128)[:, :, step * 512:(step + 1) * 512],
                           deps=[ev_xnt_free] + k.ev_xn_dma[step * 4:step * 4 + 4])
            st["ev_xnt"] = ev_xnt
        if step == 0:
            for mg in range(16):
                emit_win_m(0, mg, chunkloop=False)
        else:
            ti_c = step - 1
            for tcl in range(TPC):
                ci = ti_c * TPC + tcl
                emit_chunk_start(ci)
                for gg in range(8):
                    emit_group(ci, gg)
                    if INTERLEAVE and step < NT and gg % 2 == 1:
                        emit_win_m(step, (tcl * 4 + gg // 2), chunkloop=True)
                if k.debug and k.stage == 3 and ci in k.dbg_ci:
                    ixd = k.dbg_ci.index(ci)
                    p.dma("sync", k.dbg_y[ixd * 128:(ixd + 1) * 128, 0:2048],
                          k.y_bf[:, 0:4096].bitcast(F32), deps=[ev_y_evac])
                    p.dma("sync", k.dbg_st[ixd * 128:(ixd + 1) * 128, :], k.state[:],
                          deps=[ev_state_upd[7]])
                emit_chunk_end(ci)
            emit_rstd_wout(ti_c)
        if step < NT:
            p.op("scalar", lambda s: s.preload_act_table(AF.Silu))
            for mg in range(16 if (step == 0 or INTERLEAVE) else 0, 34):
                emit_win_m(step, mg, chunkloop=False)
            bt = BK(2, 0)
            evbt = None
            for tcl in range(4):
                evbt = p.op("tensor",
                            lambda t, tcl=tcl: t.transpose(
                                P[2][:, tcl * 64:(tcl + 1) * 64].bitcast(BF16),
                                k.bf_bf[:, tcl * 128:(tcl + 1) * 128], k.ident_bf[:]),
                            deps=([ev_bf] + pb.wdeps(bt)) if tcl == 0 else [], inc=(tcl == 3))
            pb.wrote(evbt, bt)
            ev_btok = p.op("vector", lambda v: v.tensor_copy(k.btok_bf[:],
                                                             P[2][:, 0:256].bitcast(BF16)),
                           deps=pb.rdeps(bt) + [evbt], inc=True)
            pb.read(ev_btok, bt)
            st["ev_btok"] = ev_btok
            if k.debug and k.stage == 2 and step == 0:
                p.dma("sync", k.dbg_xtok, k.buf2[:], deps=[eve, ev_btok])
            emit_prep(step)
            ev_rh_free = [ev_wob_free[0], ev_wob_free[1], st.get("ev_spl"),
                          st.get("ev_lnd"), st.get("ev_wt"), st.get("ev_w2")]

    k.ev_wout_done = ev_wout_done
    k.ev_ssm_last_dv = ev_tT[3]
    k.ev_Dmm_last = ev_Dmm_prev
    k.ev_attst_free = ev_attst_free
    k.ev_arz_free2 = [evt_, ev_ssq, eve8, ev_tT[3]]
    k.ev_zrd_free = list(ev_zrd_free)
    k.ev_wib_free = list(ev_wib_free)
    k.ev_xnt_free = ev_xnt_free
    k.ev_zstage_free = [e for e in zdma if e is not None]
    k.ev_spl_last = ev_spl if ev_spl is not None else st.get("ev_spl")
    k.ev_wt_last = ev_wt if ev_wt is not None else st.get("ev_wt")
    k.ev_rstd_last = ev_wibdt_free
    k.ev_rstdbc_last = ev_rstdbc


def phase3(k):
    p, pb, P = k.p, k.pb, k.P
    ev_eb = p.dma("gpsimd", k.eb_bf[:], k.exp_b,
                  deps=[k.ev_attst_free[0], k.ev_attst_free[1], k.ev_spl_last, k.ev_wt_last])
    ev_web_free = [k.ev_zrd_free[0], k.ev_zrd_free[1]]
    ev_rhs_free = [k.ev_Dmm_last] * 8
    ev_attrd_free = [None, None]
    ev_outst_free = [None, None]
    ev_u2_free = [None, None]
    ev_outf_free = k.ev_wout_done
    ev_xng_dep = [k.ev_wib_free[0], k.ev_wib_free[1], k.ev_xnt_free, k.ev_rstd_last,
                  k.ev_xn_dma[-1]]
    ev_g8_dep = list(k.ev_arz_free2)
    ev_asm_last = None
    for tg in range(NTG):
        ev_xng = p.dma("sync", k.xn_tg.rearrange("p (kk t) -> p kk t", t=TG),
                       k.xn_scr.rearrange("(kk p) t -> p kk t", p=128)[:, :, tg * TG:(tg + 1) * TG],
                       deps=ev_xng_dep)
        ev_g8 = None
        for e in range(NE):
            bk = (BK(0, 0), BK(0, 1))
            evm = None
            for half in range(2):
                evm = p.op("tensor",
                           lambda t, e=e, half=half, tg=tg: t.matmul(
                               P[0][:, half * 512:(half + 1) * 512], k.sel8[:, e * 128:(e + 1) * 128],
                               k.gates_bf[:, tg * 1024 + half * 512: tg * 1024 + (half + 1) * 512],
                               start=True, stop=True),
                           deps=([k.ev_gts] + pb.wdeps(*bk)) if half == 0 else [], inc=(half == 1))
            pb.wrote(evm, *bk)
            ev_g8 = p.op("vector", lambda v, e=e: v.tensor_copy(
                k.g8_bf[:, e * 1024:(e + 1) * 1024], P[0][:]),
                deps=pb.rdeps(*bk) + [evm] + (ev_g8_dep if (tg == 0 and e == 0) else []),
                inc=True)
            pb.read(ev_g8, *bk)
        ev_g8_dep = []
        ev_evac_last = None
        for mg in range(8):
            pair = (0, 1) if mg % 2 == 0 else (2, 3)
            bksall = (BK(pair[0], 0), BK(pair[0], 1), BK(pair[1], 0), BK(pair[1], 1))
            evkk = None
            for e in range(NE):
                sle = e % 2
                evwd = p.dma("gpsimd", k.web[sle].rearrange("p (kk m) -> p kk m", m=256),
                             k.exp_w[e].rearrange("(kk p) m -> p kk m", p=128)[:, :, mg * 256:(mg + 1) * 256],
                             deps=[ev_web_free[sle]])
                for kk in range(16):
                    slr = kk % 8
                    evrh = p.op("vector",
                                lambda v, kk=kk, e=e, slr=slr: v.tensor_tensor(
                                    out=k.rhs_ek[slr][:], in0=k.xn_tg[:, kk * 1024:(kk + 1) * 1024],
                                    in1=k.g8_bf[:, e * 1024:(e + 1) * 1024], op=ALU.mult),
                                deps=[ev_rhs_free[slr], ev_g8 if (kk == 0 and e == 0) else None,
                                      ev_xng if (kk == 0 and e == 0) else None],
                                inc=True)
                    ev_rhs_free[slr] = None
                    first = (e == 0 and kk == 0)
                    evmm = None
                    for mi in range(2):
                        for half in range(2):
                            evmm = p.op("tensor",
                                        lambda t, pair=pair, mi=mi, half=half, kk=kk, sle=sle,
                                        slr=slr, e=e:
                                        t.matmul(P[pair[mi]][:, half * 512:(half + 1) * 512],
                                                 k.web[sle][:, kk * 256 + mi * 128: kk * 256 + mi * 128 + 128],
                                                 k.rhs_ek[slr][:, half * 512:(half + 1) * 512],
                                                 start=(e == 0 and kk == 0), stop=False),
                                        deps=([evrh, evwd] + (pb.wdeps(*bksall) if first else []))
                                        if (mi == 0 and half == 0) else [],
                                        inc=(mi == 1 and half == 1 and kk % 4 == 3))
                    if kk % 4 == 3:
                        evkk = evmm
                        for back in range(4):
                            ev_rhs_free[(kk - back) % 8] = evmm
                ev_web_free[sle] = evkk
            evb = None
            for mi in range(2):
                m = mg * 2 + mi
                for half in range(2):
                    evb = p.op("tensor",
                               lambda t, pair=pair, mi=mi, half=half, m=m, tg=tg:
                               t.matmul(P[pair[mi]][:, half * 512:(half + 1) * 512],
                                        k.eb_bf[:, m * 128:(m + 1) * 128],
                                        k.gates_bf[:, tg * 1024 + half * 512: tg * 1024 + (half + 1) * 512],
                                        start=False, stop=True),
                               deps=[ev_eb if (tg == 0 and mg == 0 and mi == 0 and half == 0) else None],
                               inc=(mi == 1 and half == 1))
            pb.wrote(evb, *bksall)
            for mi in range(2):
                m = mg * 2 + mi
                sla = m % 2
                evar = p.dma("sync", k.att_rd[sla][:],
                             k.att_scr[m * 128:(m + 1) * 128, tg * 1024:(tg + 1) * 1024],
                             deps=[ev_attrd_free[sla],
                                   k.ev_att_w[2 * tg * 16 + m],
                                   k.ev_att_w[(2 * tg + 1) * 16 + m],
                                   k.ev_rstdbc_last if (tg == 0 and m < 2) else None]
                             + (k.ev_arz_free2 if (tg == 0 and m < 2) else []))
                eve = p.op("vector",
                           lambda v, pair=pair, mi=mi, m=m, sla=sla: v.tensor_tensor(
                               out=k.outf[:, m * 1024:(m + 1) * 1024], in0=P[pair[mi]][:],
                               in1=k.att_rd[sla][:], op=ALU.add),
                           deps=pb.rdeps(BK(pair[mi], 0), BK(pair[mi], 1)) +
                           [evb, evar,
                            ev_outf_free if (tg == 0 and mg < 2 and mi == 0) else None],
                           inc=True)
                pb.read(eve, BK(pair[mi], 0), BK(pair[mi], 1))
                ev_attrd_free[sla] = eve
                ev_evac_last = eve
        ev_outf_free = None
        for tcl in range(8):
            tok0 = tg * 1024 + tcl * 128
            slu = tcl % 2
            ev_u2 = p.dma("sync", k.u_sb[slu][:], k.u[tok0:tok0 + 128, :],
                          deps=[ev_u2_free[slu],
                                (k.ev_ssm_last_dv if (tg == 0 and tcl < 2) else None)])
            ev_q = []
            for q in range(4):
                slq = q % 2
                bk = BK(slq, 0)
                pT = P[slq][:, 0:256].bitcast(BF16)
                evt = None
                for j in range(4):
                    m = q * 4 + j
                    evt = p.op("tensor",
                               lambda t, pT=pT, j=j, m=m, tcl=tcl: t.transpose(
                                   pT[:, j * 128:(j + 1) * 128],
                                   k.outf[:, m * 1024 + tcl * 128: m * 1024 + (tcl + 1) * 128],
                                   k.ident_bf[:]),
                               deps=([ev_evac_last] + pb.wdeps(bk)) if j == 0 else [],
                               inc=(j == 3))
                pb.wrote(evt, bk)
                eve = p.op("vector",
                           lambda v, pT=pT, q=q, slu=slu: v.tensor_tensor(
                               out=k.out_st[slu][:, q * 512:(q + 1) * 512], in0=pT,
                               in1=k.u_sb[slu][:, q * 512:(q + 1) * 512], op=ALU.add),
                           deps=pb.rdeps(bk) + [evt, ev_u2 if q == 0 else None,
                                                ev_outst_free[slu] if q == 0 else None],
                           inc=True)
                pb.read(eve, bk)
                ev_q.append(eve)
            evod = p.dma("sync", k.out[tok0:tok0 + 128, :], k.out_st[slu][:], deps=ev_q)
            ev_outst_free[slu] = evod
            ev_u2_free[slu] = ev_q[-1]
            ev_asm_last = ev_q[-1]
        ev_xng_dep = [ev_asm_last]


from concourse.bass_utils import run_bass_kernel_spmd


def kernel(L0_attn_u, L0_attn_W_in, L0_attn_conv_w, L0_attn_conv_b, L0_attn_dt_bias,
           L0_attn_A_log, L0_attn_D, L0_attn_norm_w, L0_attn_W_out, L0_ffn_gate_w,
           L0_ffn_expert_w, L0_ffn_expert_b):
    common = dict(
        w_in=np.ascontiguousarray(np.asarray(L0_attn_W_in, dtype=np.float32)),
        conv_w=np.ascontiguousarray(np.asarray(L0_attn_conv_w, dtype=np.float32)),
        conv_b=np.ascontiguousarray(np.asarray(L0_attn_conv_b, dtype=np.float32)),
        dt_bias=np.ascontiguousarray(np.asarray(L0_attn_dt_bias, dtype=np.float32)),
        a_log=np.ascontiguousarray(np.asarray(L0_attn_A_log, dtype=np.float32)),
        dvec=np.ascontiguousarray(np.asarray(L0_attn_D, dtype=np.float32)),
        norm_w=np.ascontiguousarray(np.asarray(L0_attn_norm_w, dtype=np.float32)),
        w_out=np.ascontiguousarray(np.asarray(L0_attn_W_out, dtype=np.float32)),
        gate_w=np.ascontiguousarray(np.asarray(L0_ffn_gate_w, dtype=np.float32)),
        exp_w=np.ascontiguousarray(np.asarray(L0_ffn_expert_w, dtype=np.float32)),
        exp_b=np.ascontiguousarray(np.asarray(L0_ffn_expert_b, dtype=np.float32)),
    )
    uu = np.ascontiguousarray(np.asarray(L0_attn_u, dtype=np.float32))
    in_maps = [dict(u=np.ascontiguousarray(uu[c]), **common) for c in range(8)]
    nc = build_nc(debug=0, stage=9)
    res = run_bass_kernel_spmd(nc, in_maps, core_ids=list(range(8)))
    return np.stack([np.asarray(res.results[c]["out"], dtype=np.float32) for c in range(8)], axis=0)
